# version 04 (final)
# baseline (speedup 1.0000x reference)
import numpy as np
import contextlib
import concourse.bass as bass
import concourse.mybir as mybir
from concourse.bass_utils import run_bass_kernel_spmd

F32 = mybir.dt.float32
BF16 = mybir.dt.bfloat16
I32 = mybir.dt.int32
AF = mybir.ActivationFunctionType
ALU = mybir.AluOpType
AX = mybir.AxisListType

FULL_CFG = dict(D=2048, I=8192, T=1024, CTX=1024, NH=32)

PI = float(np.float32(np.pi))
TWO_PI_HI = float(np.float32(2.0 * np.pi))
TWO_PI_LO = float(np.float32(2.0 * np.pi - float(np.float32(2.0 * np.pi))))
HALF_PI_HI = float(np.float32(0.5 * np.pi))
HALF_PI_LO = float(np.float32(0.5 * np.pi - float(np.float32(0.5 * np.pi))))


class _MockInst:
    def then_inc(self, *a, **k):
        return self


class _MockEng:
    def __getattr__(self, name):
        def f(*a, **k):
            return _MockInst()
        return f


class Track:
    def __init__(self, name, dma):
        self.name = name
        self.dma = dma
        self.sem = None
        self.n = 0


class P:
    def __init__(self, tracks, emit, frozen=None):
        self.tracks = {t.name: t for t in tracks}
        self.emit = emit
        self.cp = {}
        self.frozen = frozen or {}
        self.events = {}
        self.cur = None

    def engine(self, name):
        self.cur = name
        self.events.setdefault(name, [])

    def inc(self, inst, tname, name=None, k=None):
        t = self.tracks[tname]
        kk = k if k is not None else (16 if t.dma else 1)
        t.n += kk
        if name is not None:
            assert name not in self.cp, f"dup checkpoint {name}"
            self.cp[name] = t.n
        if self.emit:
            inst.then_inc(t.sem, kk)
        else:
            self.events[self.cur].append(("inc", tname, kk))
        return t.n

    def wait(self, eng, tname, name):
        if self.emit:
            eng.wait_ge(self.tracks[tname].sem, self.frozen[name])
        else:
            self.events[self.cur].append(("wait", tname, name))

    def wait_total(self, eng, tname):
        if self.emit:
            eng.wait_ge(self.tracks[tname].sem, self.frozen["TOTAL_" + tname])
        else:
            self.events[self.cur].append(("wait_total", tname, None))

    def fence(self, eng, inst, tname, name=None):
        v = self.inc(inst, tname, name)
        if self.emit:
            eng.wait_ge(self.tracks[tname].sem, v)
        else:
            self.events[self.cur].append(("wait_val", tname, v))


def _check_deadlock(P0):
    vals = {t: 0 for t in P0.tracks}
    pcs = {e: 0 for e in P0.events}
    progress = True
    while progress:
        progress = False
        for e, evs in P0.events.items():
            while pcs[e] < len(evs):
                kind, tname, x = evs[pcs[e]]
                if kind == "inc":
                    vals[tname] += x
                elif kind == "wait":
                    if vals[tname] < P0.cp[x]:
                        break
                elif kind == "wait_val":
                    if vals[tname] < x:
                        break
                else:
                    if vals[tname] < P0.cp["TOTAL_" + tname]:
                        break
                pcs[e] += 1
                progress = True
    stuck = {}
    for e, evs in P0.events.items():
        if pcs[e] < len(evs):
            kind, tname, x = evs[pcs[e]]
            need = P0.cp[x] if kind == "wait" else P0.cp.get("TOTAL_" + tname)
            stuck[e] = (f"pc {pcs[e]}/{len(evs)}", kind, tname, x,
                        "need", need, "have", vals[tname])
    if stuck:
        raise RuntimeError(f"build-time deadlock: {stuck}")


class Arena:
    def __init__(self, nc, base_addr, size, tag):
        self.nc = nc
        self.base = base_addr
        self.size = size
        self.cur = 0
        self.tag = tag
        self.offs = {}

    def _mk(self, name, shape, dtype, addr):
        h = self.nc.virtual_sbuf_tensor(name, list(shape), dtype)
        self.nc._tensor(h.name, list(h.shape), h.dtype, type="SB")
        mloc = self.nc.lookup_mloc(h)
        mloc.allocated = True
        mloc.addr = addr
        return h

    def T(self, name, shape, dtype=F32):
        per = int(np.prod(shape[1:])) * mybir.dt.size(dtype)
        per = (per + 31) // 32 * 32
        assert self.cur + per <= self.size, \
            f"arena {self.tag} overflow at {name}: {self.cur + per} > {self.size}"
        h = self._mk(name, shape, dtype, self.base + self.cur)
        self.offs[name] = self.cur
        self.cur += per
        return h

    def at(self, off, name, shape, dtype=F32):
        per = int(np.prod(shape[1:])) * mybir.dt.size(dtype)
        assert off + per <= self.size
        h = self._mk(name, shape, dtype, self.base + off)
        self.offs[name] = off
        return h


_PROG_CACHE = {}


def build_program(cfg):
    key = tuple(sorted(cfg.items()))
    if key in _PROG_CACHE:
        return _PROG_CACHE[key]

    D, I, T, CTX, NH = cfg["D"], cfg["I"], cfg["T"], cfg["CTX"], cfg["NH"]
    L = T + CTX
    DC = D // 128
    TB = T // 128
    CB = CTX // 128
    LB = L // 128
    ICN = I // 512
    ICI = I // 128
    NQ = D // 512
    NGR = (DC + 3) // 4
    assert NGR == NQ
    QN = 2 * max(1, DC // 8)
    QDC = DC // QN
    assert QDC * QN == DC
    NSB = max(1, TB // 4)
    TBS = TB // NSB
    HH = max(1, D // 1024)
    HW = D // HH
    NHF = HW // 512
    HCW = min(I, 1024)
    HCH = I // HCW
    HGR = HCW // 512
    WCW = min(I, 512)
    NW_GU = (D // 128) * (I // WCW) * 2
    WDW = HW
    NWDR = ICI * HH
    WSD = min(D, 512)
    NW_D = (I // 128) * (D // WSD)
    INV_D = float(np.float32(1.0 / D))
    INV_I = float(np.float32(1.0 / I))
    INV_NW = float(np.float32(1.0 / (D * I)))
    HSG = min(4, TB)
    NHSG = TB // HSG

    nc = bass.Bass("TRN2", target_bir_lowering=False, debug=False,
                   detect_race_conditions=False)

    u_own = nc.dram_tensor("u_own", [T, D], F32, kind="ExternalInput")
    u_ctx = nc.dram_tensor("u_ctx", [CTX, D], F32, kind="ExternalInput")
    wgT = nc.dram_tensor("wgT", [D, I], F32, kind="ExternalInput")
    wuT = nc.dram_tensor("wuT", [D, I], F32, kind="ExternalInput")
    wdT = nc.dram_tensor("wdT", [I, D], F32, kind="ExternalInput")
    lnw_d = nc.dram_tensor("lnw_d", [1, I], F32, kind="ExternalInput")
    bre_d = nc.dram_tensor("bre_d", [NH, D], F32, kind="ExternalInput")
    bim_d = nc.dram_tensor("bim_d", [NH, D], F32, kind="ExternalInput")
    cre_d = nc.dram_tensor("cre_d", [D, NH], F32, kind="ExternalInput")
    cim_d = nc.dram_tensor("cim_d", [D, NH], F32, kind="ExternalInput")
    dvec_d = nc.dram_tensor("dvec_d", [1, D], F32, kind="ExternalInput")
    lunc_d = nc.dram_tensor("lunc_d", [NH, 1], F32, kind="ExternalInput")
    lim_d = nc.dram_tensor("lim_d", [NH, 1], F32, kind="ExternalInput")
    logd_d = nc.dram_tensor("logd_d", [NH, 1], F32, kind="ExternalInput")
    out_d = nc.dram_tensor("out", [T, D], F32, kind="ExternalOutput")
    h_scr = nc.dram_tensor("h_scr", [T, I], F32, kind="Internal")
    s_scr = nc.dram_tensor("s_scr", [T, D], F32, kind="Internal")
    wd_tern = nc.dram_tensor("wd_tern", [I, D], BF16, kind="Internal")
    lnw_scr = nc.dram_tensor("lnw_scr", [128, I], F32, kind="Internal")
    DEBUG = bool(cfg.get("DEBUG", 0))
    DBGMD = bool(cfg.get("DBGMD", 0))
    POISON = bool(cfg.get("POISON", 0))
    if DEBUG:
        dbg_xn = nc.dram_tensor("dbg_xn", [T, D], F32, kind="ExternalOutput")
        dbg_qneg = nc.dram_tensor("dbg_qneg", [T, D], BF16,
                                  kind="ExternalOutput")
        dbg_xint = nc.dram_tensor("dbg_xint", [128, DC * T], BF16,
                                  kind="ExternalOutput")
        dbg_s = nc.dram_tensor("dbg_s", [T, D], F32, kind="ExternalOutput")
        dbg_h = nc.dram_tensor("dbg_h", [T, I], F32, kind="ExternalOutput")
        dbg_wdt = nc.dram_tensor("dbg_wdt", [I, D], BF16,
                                 kind="ExternalOutput")
        dbg_lnw = nc.dram_tensor("dbg_lnw", [128, I], F32,
                                 kind="ExternalOutput")
        dbg_stats = nc.dram_tensor("dbg_stats", [128, 8 * TB + 40], F32,
                                   kind="ExternalOutput")
        dbg_ssm = nc.dram_tensor("dbg_ssm", [NH, 7 * L], F32,
                                 kind="ExternalOutput")
        dbg_par = nc.dram_tensor("dbg_par", [NH, 26], F32,
                                 kind="ExternalOutput")
    if DEBUG or DBGMD:
        dbg_wsd = nc.dram_tensor("dbg_wsd", [128, 260], F32,
                                 kind="ExternalOutput")

    A = lambda n, s, d=F32: nc.alloc_sbuf_tensor(n, s, d)
    ones1 = A("ones1", [1, 128])
    ones128 = A("ones128", [128, 1])
    ident_bf = A("ident_bf", [128, 128], BF16)
    identw = A("identw", [128, 128])
    identNH_bf = A("identNH_bf", [NH, NH], BF16)
    par = A("par", [NH, 26])
    pc = {k: i for i, k in enumerate(
        ["lunc", "lim", "logd", "e1", "sp", "lre", "dlt", "lred", "th",
         "r", "thw", "cth", "thc", "sth", "wre", "wim", "den", "dinv",
         "nre", "t1", "t2", "t3", "t4"])}
    PV = lambda k: par[:, pc[k]:pc[k] + 1]
    invsx = A("invsx", [128, TB])
    sumsq = A("sumsq", [128, TB])
    rm2 = A("rm2", [128, TB])
    cg_l = [A(f"cg{i}", [128, 1]) for i in range(TB)]
    cu = A("cu", [128, TB])
    rsq = A("rsq", [128, TB])
    cdq_l = [A(f"cdq{i}", [128, 1]) for i in range(TB)]
    stat1 = A("stat1", [128, 20])
    wsum_g = A("wsum_g", [128, 1])
    wsum_u = A("wsum_u", [128, 1])
    wsum_d = A("wsum_d", [128, 1])
    wsums_gu = A("wsums_gu", [128, 1024])
    wsums_d = A("wsums_d", [128, 256])
    swg_bc = A("swg_bc", [128, 1])
    swu_bc = A("swu_bc", [128, 1])
    swd_bc = A("swd_bc", [128, 1])
    mg_bc = A("mg_bc", [128, 1])
    mu_bc = A("mu_bc", [128, 1])
    md_bc = A("md_bc", [128, 1])
    sc11 = A("sc11", [1, 8])
    chalf_n = A("chalf_n", [128, 1])
    chalf_p = A("chalf_p", [128, 1])
    ln_sc = [A(f"ln_sc{i}", [128, 1]) for i in range(2)]
    ln_nb = [A(f"ln_nb{i}", [128, 1]) for i in range(2)]
    q_sc = A("q_sc", [128, 1])
    lnw_sl = [A(f"lnw_sl{i}", [1, 512]) for i in range(2)]
    dummy = A("dummy_sb", [128, 2048])
    bretT = A("bretT", [128, DC * NH], BF16)
    bimtT = A("bimtT", [128, DC * NH], BF16)
    ctre_bf = A("ctre_bf", [NH, D], BF16)
    ctim_bf = A("ctim_bf", [NH, D], BF16)
    ctile = A("ctile", [128, NH])
    ctile_bf = A("ctile_bf", [128, NH], BF16)

    pb = [nc.alloc_psum_tensor(f"pb{i}", [128, 512], F32) for i in range(8)]
    vts = [pb[6][:].bitcast(BF16), pb[7][:].bitcast(BF16)]

    remaining = nc.sbuf_bytes_remaining - 512
    scratch = nc.alloc_sbuf_tensor("scratch", [128, remaining], mybir.dt.uint8)
    sbase = nc.lookup_mloc(scratch).addr
    XZ = Arena(nc, sbase, remaining, "XZ")
    x_intT = XZ.T("x_intT", [128, DC * T], BF16)
    ph_base = sbase + XZ.cur
    ph_size = remaining - XZ.cur
    a0 = Arena(nc, ph_base, ph_size, "P0")
    aG = Arena(nc, ph_base, ph_size, "G")
    aD = Arena(nc, sbase, remaining, "D")
    if POISON:
        aPZ = Arena(nc, sbase, remaining, "PZ")
        PZC = remaining // 4
        pzf = aPZ.at(0, "pzf", [128, PZC])
        pzb = aPZ.at(0, "pzb", [128, remaining // 2], BF16)

    u_stage = [a0.T(f"u_stage{i}", [128, D]) for i in range(2)]
    xn1 = a0.T("xn1", [128, D])
    xbf_q = [a0.T(f"xbf_q{i}", [128, 512], BF16) for i in range(2)]
    xnT_tile = [a0.T(f"xnT_tile{i}", [128, D], BF16) for i in range(2)]
    qy = a0.T("qy", [128, 512])
    qi32 = a0.T("qi32", [128, 512], I32)
    qtf = a0.T("qtf", [128, 512], BF16)
    qd = a0.T("qd", [128, 512])
    qa = a0.T("qa", [128, 512], BF16)
    qneg = a0.T("qneg", [128, D], BF16)
    dbcast = a0.T("dbcast", [128, D])
    rotc = a0.T("rotc", [NH, L])
    rots = a0.T("rots", [NH, L])
    bu = a0.T("bu", [NH, 2 * L])
    vt_t = a0.T("vt_t", [NH, 2 * L])
    vv1 = a0.T("vv1", [NH, L])
    vv2 = a0.T("vv2", [NH, L])
    xs_bf = a0.T("xs_bf", [NH, 2 * T], BF16)
    s_stage = a0.T("s_stage", [128, D])
    sd_scr = a0.T("sd_scr", [128, D])
    wst = [a0.T(f"wst{i}", [128, WCW]) for i in range(2)]
    bretile = a0.at(a0.offs["bu"], "bretile", [NH, D])
    bimtile = a0.at(a0.offs["bu"] + 4 * L, "bimtile", [NH, D])
    assert a0.offs["qa"] + 1024 - a0.offs["qy"] >= 4 * L
    atile = a0.at(a0.offs["qy"], "atile", [NH, L])
    bre_bf = a0.at(a0.offs["s_stage"], "bre_bf", [NH, D], BF16)
    bim_bf = a0.at(a0.offs["s_stage"] + 2 * D, "bim_bf", [NH, D], BF16)
    bb3 = sd_scr
    bb1, bb2 = vv1, vv2
    zt = bu
    xs = vt_t

    wguf = [aG.T(f"wguf{i}", [128, QDC * 512]) for i in range(2)]
    tern1 = [aG.T(f"tern1_{i}", [128, QDC * 512]) for i in range(2)]
    tern2 = [aG.T(f"tern2_{i}", [128, QDC * 512]) for i in range(2)]
    wg_bf = [aG.T(f"wg_bf{i}", [128, DC * 512], BF16) for i in range(2)]
    wu_bf = [aG.T(f"wu_bf{i}", [128, DC * 512], BF16) for i in range(2)]
    h_stage = [aG.T(f"h_stage{i}", [128, HSG * 512]) for i in range(2)]
    g_stage = [aG.T(f"g_stage{i}", [128, 512]) for i in range(2)]
    hsq = [aG.T(f"hsq{i}", [128, 512]) for i in range(2)]
    hlw = None
    lnwq_bc = [aG.T(f"lnwq_bc{i}", [128, 512]) for i in range(2)]

    qhT = aD.T("qhT", [128, ICI * TBS * 128], BF16)
    lnr_sl = [aD.T(f"lnr_sl{i}", [128, HCW]) for i in range(2)]
    hrd = [aD.T(f"hrd{i}", [128, HCW]) for i in range(2)]
    hy = [aD.T(f"hy{i}", [128, HCW]) for i in range(2)]
    hq_i32 = [aD.T(f"hq_i32{i}", [128, HCW], I32) for i in range(2)]
    hq_tf = [aD.T(f"hq_tf{i}", [128, HCW], BF16) for i in range(2)]
    hqd = aD.T("hqd", [128, HCW])
    hqa = aD.T("hqa", [128, HCW], BF16)
    qh_tok = [aD.T(f"qh_tok{i}", [128, HCW], BF16) for i in range(2)]
    NWDB = 4
    wdb = [aD.T(f"wdb{i}", [128, HW], BF16) for i in range(NWDB)]
    wdf = [aD.T(f"wdf{i}", [128, WDW]) for i in range(2)]
    wdsg1 = aD.T("wdsg1", [128, WDW])
    wdsg2 = aD.T("wdsg2", [128, WDW])
    wdt_bf = [aD.T(f"wdt_bf{i}", [128, WDW], BF16) for i in range(2)]
    s_sl = [aD.T(f"s_sl{i}", [128, 512]) for i in range(2)]
    x_sl = [aD.T(f"x_sl{i}", [128, 512]) for i in range(2)]
    go_stage = [aD.T(f"go_stage{i}", [128, 512]) for i in range(2)]
    o_stage = [aD.T(f"o_stage{i}", [128, 512]) for i in range(2)]

    tracks = [Track(n, d) for n, d in [
        ("s_par", True), ("s_u", True), ("s_wst", True), ("s_wstd", True),
        ("s_wgu", True), ("s_wdf", True), ("s_wdb", True), ("s_hrd", True),
        ("s_sxr", True), ("s_hw", True), ("s_sw", True), ("s_sd", True),
        ("s_sr", True), ("s_wtw", True), ("s_ow", True), ("s_lnw", True),
        ("s_lbw", True), ("s_lnr", True), ("s_dbg", True),
        ("c_v", False), ("c_a", False), ("c_p", False), ("c_g", False),
    ]]

    GEND = f"gu_mm_{ICN * TB - 1}"
    NW_TOT = NW_GU + NW_D
    NW_PER = (NW_TOT + LB - 1) // LB

    p0seq = []
    for lb in range(LB):
        for g in range(NGR):
            p0seq.append(("xn", lb, g))
        if lb >= CB:
            for g in range(NGR):
                p0seq.append(("xq", lb - CB, g))
    p0par = [it[2] % 2 for it in p0seq]
    p0ev = {("xn", lb, g): f"xnT_ev_{lb}_{g}" for lb in range(LB)
            for g in range(NGR)}
    p0ev.update({("xq", tb, g): f"xqT_ev_{tb}_{g}" for tb in range(TB)
                 for g in range(NGR)})

    def p0guard(i):
        j = i - 1
        while j >= 0 and p0par[j] != p0par[i]:
            j -= 1
        return p0ev[p0seq[j]] if j >= 0 else None

    p0q = []
    for lb in range(LB):
        for g in range(NGR):
            p0q.append((lb, g))


    def body_sync(E, Q):
        Q.engine("sync")
        if POISON:
            Q.wait(E, "s_lbw", "poison_done")
        for dst, src, nm in [
            (PV("lunc"), lunc_d.ap(), None), (PV("lim"), lim_d.ap(), None),
            (PV("logd"), logd_d.ap(), None),
            (bretile[:], bre_d.ap(), None), (bimtile[:], bim_d.ap(), None),
            (dbcast[0:1, :], dvec_d.ap(), "par_core"),
        ]:
            Q.inc(E.dma_start(out=dst, in_=src), "s_par", nm)
        for dc in range(DC):
            if dc > 0:
                Q.wait(E, "c_v", f"ctile_used_im_{dc - 1}")
            Q.inc(E.dma_start(out=ctile[:],
                              in_=cre_d.ap()[dc * 128:(dc + 1) * 128, :]),
                  "s_par", f"cre_{dc}")
            Q.wait(E, "c_v", f"ctile_used_re_{dc}")
            Q.inc(E.dma_start(out=ctile[:],
                              in_=cim_d.ap()[dc * 128:(dc + 1) * 128, :]),
                  "s_par", f"cim_{dc}")

        def wst_dma(k):
            if k < NW_GU:
                wt = wgT if k < NW_GU // 2 else wuT
                kk = k % (NW_GU // 2)
                rb, q = kk // (I // WCW), kk % (I // WCW)
                src = wt.ap()[rb * 128:(rb + 1) * 128,
                              q * WCW:(q + 1) * WCW]
            else:
                kk = k - NW_GU
                rb, q = kk // (D // WSD), kk % (D // WSD)
                src = wdT.ap()[rb * 128:(rb + 1) * 128,
                               q * WSD:(q + 1) * WSD]
            if k >= 2:
                p = k - 2
                nm = f"wstp_{p}" if p < NW_GU else f"wstdp_{p - NW_GU}"
                Q.wait(E, "c_a", nm)
            Q.inc(E.dma_start(out=wst[k % 2][:], in_=src),
                  "s_wst", f"wst_{k}")

        for lb in range(LB):
            if lb >= 2:
                Q.wait(E, "c_a", f"xn_done_{lb - 2}")
            src = (u_ctx.ap()[lb * 128:(lb + 1) * 128, :] if lb < CB
                   else u_own.ap()[(lb - CB) * 128:(lb - CB) * 128 + 128, :])
            Q.inc(E.dma_start(out=u_stage[lb % 2][:], in_=src), "s_u", f"u_{lb}")
            for k in range(NW_PER * lb, min(NW_PER * (lb + 1), NW_TOT)):
                wst_dma(k)
        for k in range(NW_PER * LB, NW_TOT):
            wst_dma(k)
        Q.wait(E, "c_v", f"s_asm_{TB - 1}")
        Q.wait(E, "c_p", f"y_mm_{TB - 1}")
        Q.wait(E, "s_sw", f"sw_{TB - 1}")
        if DEBUG:
            dssm = dbg_ssm.ap()
            for i, (tt, w) in enumerate([(rotc, 1), (rots, 1), (atile, 1),
                                         (bu, 2), (vt_t, 2)]):
                pass
            Q.inc(E.dma_start(out=dssm[:, 0:L], in_=rotc[:]), "s_par")
            Q.inc(E.dma_start(out=dssm[:, L:2 * L], in_=rots[:]), "s_par")
            Q.inc(E.dma_start(out=dssm[:, 2 * L:3 * L], in_=atile[:]), "s_par")
            Q.inc(E.dma_start(out=dssm[:, 3 * L:5 * L], in_=bu[:]), "s_par")
            Q.inc(E.dma_start(out=dssm[:, 5 * L:7 * L], in_=vt_t[:]),
                  "s_par")
            Q.inc(E.dma_start(out=dbg_par.ap(), in_=par[:],
                              allow_slow_non_contiguous=True),
                  "s_par", "dbgssm")
            Q.wait(E, "s_par", "dbgssm")
        for ic in range(ICN):
            if ic >= 2:
                Q.wait(E, "c_p", f"lnwq_mm_{ic - 2}")
            Q.inc(E.dma_start(out=lnw_sl[ic % 2][:],
                              in_=lnw_d.ap()[0:1, ic * 512:(ic + 1) * 512]),
                  "s_lnw", f"lnw_{ic}")
            for mi in range(2):
                wt = wgT if mi == 0 else wuT
                for q in range(QN):
                    gqi = (ic * 2 + mi) * QN + q
                    if gqi >= 2:
                        Q.wait(E, "c_a", f"sgn_gu_{gqi - 2}")
                    src3 = wt.ap().rearrange("(dc p) i -> p dc i", p=128)[
                        :, q * QDC:(q + 1) * QDC, ic * 512:(ic + 1) * 512]
                    dst3 = wguf[gqi % 2][:].rearrange("p (dc f) -> p dc f",
                                                      f=512)
                    Q.inc(E.dma_start(out=dst3, in_=src3), "s_wgu",
                          f"wgu_{gqi}")
        for sb in range(NSB):
            for ch in range(TBS * HCH):
                tbp, c = ch // HCH, ch % HCH
                tb = sb * TBS + tbp
                gi = sb * TBS * HCH + ch
                if gi == 0:
                    Q.wait(E, "c_p", GEND)
                    Q.wait(E, "s_lbw", f"lbw_{ICN - 1}")
                    if DEBUG:
                        Q.wait(E, "s_hw", f"hw_{ICN - 1}_{NHSG - 1}")
                        Q.wait(E, "s_sw", f"sw_{TB - 1}")
                        Q.inc(E.dma_start(out=dbg_xint.ap(), in_=x_intT[:]),
                              "s_par")
                        Q.inc(E.dma_start(out=dbg_s.ap(), in_=s_scr.ap()),
                              "s_par")
                        Q.inc(E.dma_start(out=dbg_h.ap(), in_=h_scr.ap()),
                              "s_par")
                        Q.inc(E.dma_start(out=dbg_lnw.ap(), in_=lnw_scr.ap()),
                              "s_par")
                        dsts = dbg_stats.ap()
                        for i, tt in enumerate([invsx, sumsq, rm2, cu, cu]):
                            Q.inc(E.dma_start(out=dsts[:, i * TB:(i + 1) * TB],
                                              in_=tt[:],
                                              allow_slow_non_contiguous=True),
                                  "s_par")
                        off = 5 * TB
                        for i, tt in enumerate([wsum_g, wsum_u, wsum_d,
                                                swg_bc, swu_bc, mg_bc, mu_bc]):
                            Q.inc(E.dma_start(out=dsts[:, off + i:off + i + 1],
                                              in_=tt[:],
                                              allow_slow_non_contiguous=True),
                                  "s_par")
                        Q.inc(E.dma_start(out=dsts[:, off + 8:off + 28],
                                          in_=stat1[:],
                                          allow_slow_non_contiguous=True),
                              "s_par", "dbg1")
                        Q.wait(E, "s_par", "dbg1")
                if gi >= 2:
                    Q.wait(E, "c_v", f"hy_{gi - 2}")
                Q.inc(E.dma_start(out=lnr_sl[gi % 2][:],
                                  in_=lnw_scr.ap()[:, c * HCW:(c + 1) * HCW]),
                      "s_lnr", f"lnr_{gi}")
                Q.wait(E, "s_hw",
                       f"hw_{(c + 1) * HCW // 512 - 1}_{tb // HSG}")
                Q.inc(E.dma_start(out=hrd[gi % 2][:],
                                  in_=h_scr.ap()[tb * 128:(tb + 1) * 128,
                                                 c * HCW:(c + 1) * HCW]),
                      "s_hrd", f"hrd_{gi}")
            for hh in range(HH):
                if DEBUG and sb == 0 and hh == 0:
                    Q.wait(E, "s_wtw", f"wtw_{NWDR - 1}")
                    Q.wait(E, "c_v", f"dscale_{TB - 1}")
                    Q.inc(E.dma_start(out=dbg_wdt.ap(), in_=wd_tern.ap()),
                          "s_par")
                    dsts = dbg_stats.ap()
                    off = 5 * TB + 30
                    Q.inc(E.dma_start(out=dsts[:, off:off + TB], in_=rsq[:],
                                      allow_slow_non_contiguous=True),
                          "s_par")
                    Q.inc(E.dma_start(out=dsts[:, off + TB:off + TB + 1],
                                      in_=md_bc[:],
                                      allow_slow_non_contiguous=True),
                          "s_par")
                    Q.inc(E.dma_start(out=dsts[:, off + TB + 1:off + TB + 2],
                                      in_=swd_bc[:],
                                      allow_slow_non_contiguous=True),
                          "s_par", "dbg2a")
                    Q.wait(E, "s_par", "dbg2a")
                if (DEBUG or DBGMD) and sb == 0 and hh == 0:
                    Q.wait(E, "s_wtw", f"wtw_{NWDR - 1}")
                    Q.inc(E.dma_start(out=dbg_wsd.ap()[:, 0:NW_D],
                                      in_=wsums_d[:, 0:NW_D]), "s_par")
                    Q.inc(E.dma_start(out=dbg_wsd.ap()[:, 256:257],
                                      in_=wsum_d[:],
                                      allow_slow_non_contiguous=True),
                          "s_par")
                    Q.inc(E.dma_start(out=dbg_wsd.ap()[0:1, 257:258],
                                      in_=sc11[0:1, 2:3],
                                      allow_slow_non_contiguous=True),
                          "s_par")
                    Q.inc(E.dma_start(out=dbg_wsd.ap()[:, 258:259],
                                      in_=md_bc[:],
                                      allow_slow_non_contiguous=True),
                          "s_par")
                    Q.inc(E.dma_start(out=dbg_wsd.ap()[:, 259:260],
                                      in_=swd_bc[:],
                                      allow_slow_non_contiguous=True),
                          "s_par", "dbg2")
                    Q.wait(E, "s_par", "dbg2")
                for icI in range(ICI):
                    wi = (sb * HH + hh) * ICI + icI
                    if wi >= NWDB:
                        Q.wait(E, "c_p", f"dmm_{wi - NWDB}")
                    Q.wait(E, "s_wtw", f"wtw_{icI * HH + hh}")
                    Q.inc(E.dma_start(out=wdb[wi % NWDB][:],
                                      in_=wd_tern.ap()[icI * 128:(icI + 1) * 128,
                                                       hh * HW:(hh + 1) * HW]),
                          "s_wdb", f"wdb_{wi}")

    def body_gpsimd(E, Q):
        Q.engine("gpsimd")
        if POISON:
            PZVAL = 1.0e30
            step = 8192
            last = None
            for c0 in range(0, PZC, step):
                last = E.memset(pzf[:, c0:min(c0 + step, PZC)], PZVAL)
            Q.fence(E, last, "c_g", "pz_sb")
            for nb in range(T // 128):
                Q.inc(E.dma_start(out=h_scr.ap()[nb * 128:(nb + 1) * 128, :],
                                  in_=pzf[:, 0:I]), "s_lbw")
                Q.inc(E.dma_start(out=s_scr.ap()[nb * 128:(nb + 1) * 128, :],
                                  in_=pzf[:, 0:D]), "s_lbw")
            Q.inc(E.dma_start(out=lnw_scr.ap(), in_=pzf[:, 0:I]), "s_lbw")
            for nb in range(I // 128):
                nm = "poison_done" if nb == I // 128 - 1 else None
                Q.inc(E.dma_start(out=wd_tern.ap()[nb * 128:(nb + 1) * 128, :],
                                  in_=pzb[:, 0:D]), "s_lbw", nm)
        E.memset(identw[:], 0.0)
        Q.fence(E, E.affine_select(out=identw[:], in_=identw[:],
                                   compare_op=ALU.not_equal, fill=1.0, base=0,
                                   pattern=[[-1, 128]], channel_multiplier=1),
                "c_g")
        E.memset(ones1[:], 1.0)
        E.memset(ones128[:], 1.0)
        E.memset(sumsq[:], 0.0)
        E.memset(rm2[:], 0.0)
        E.memset(chalf_n[:], -0.5)
        E.memset(chalf_p[:], 0.5)
        E.memset(rotc[:, 0:1], 1.0)
        Q.inc(E.memset(rots[:, 0:1], 0.0), "c_g", "init")
        Q.wait(E, "c_v", "bbar_bf")
        Q.wait(E, "c_v", f"qch_{TB - 1}_{NQ - 1}")
        Q.inc(E.memset(atile[:], 1.0), "c_g", "atile_set")
        for ic in range(ICN):
            Q.wait(E, "c_v", f"lnwq_cp_{ic}")
            Q.inc(E.dma_start(out=lnw_scr.ap()[:, ic * 512:(ic + 1) * 512],
                              in_=lnwq_bc[ic % 2][:]), "s_lbw", f"lbw_{ic}")
            for gr in range(NHSG):
                Q.wait(E, "c_v", f"h_{ic}_{gr * HSG + HSG - 1}")
                hbi = ic * NHSG + gr
                src3 = h_stage[hbi % 2][:].rearrange("p (tb f) -> p tb f",
                                                     f=512)
                dst4 = h_scr.ap().rearrange("(tb p) (ic f) -> ic p tb f",
                                            p=128, f=512)[ic][
                    :, gr * HSG:(gr + 1) * HSG, :]
                Q.inc(E.dma_start(out=dst4, in_=src3), "s_hw", f"hw_{ic}_{gr}")
        for sb in range(NSB):
            nch = TBS * HCH
            for ch in range(nch):
                gi = sb * nch + ch
                Q.wait(E, "c_v", f"hy_{gi}")
                if gi >= 2:
                    Q.wait(E, "c_v", f"hq_done_{gi - 2}")
                E.tensor_copy(hq_i32[gi % 2][:], hy[gi % 2][:])
                Q.inc(E.tensor_copy(hq_tf[gi % 2][:], hq_i32[gi % 2][:]),
                      "c_g", f"hcvt_{gi}")

    def body_scalar(E, Q):
        Q.engine("scalar")
        if POISON:
            Q.wait(E, "s_lbw", "poison_done")
        Q.wait(E, "s_par", "par_core")
        FA = lambda inst, nm=None: Q.fence(E, inst, "c_a", nm)
        FA(E.activation(out=PV("e1"), in_=PV("lunc"), func=AF.Exp, bias=0.0,
                        scale=1.0))
        FA(E.activation(out=PV("sp"), in_=PV("e1"), func=AF.Ln, bias=1.0,
                        scale=1.0))
        FA(E.activation(out=PV("dlt"), in_=PV("logd"), func=AF.Exp,
                        bias=0.0, scale=1.0), "par_a1")
        Q.wait(E, "c_v", "par_v1")
        FA(E.activation(out=PV("r"), in_=PV("lred"), func=AF.Exp, bias=0.0,
                        scale=1.0), "par_a2")
        Q.wait(E, "c_v", "par_v2")
        FA(E.activation(out=PV("sth"), in_=PV("thw"), func=AF.Sin, bias=0.0,
                        scale=1.0))
        FA(E.activation(out=PV("cth"), in_=PV("thc"), func=AF.Sin, bias=0.0,
                        scale=1.0), "par_a3")
        qflat = 0
        for lb in range(LB):
            own = lb >= CB
            tb = lb - CB
            Q.wait(E, "s_u", f"u_{lb}")
            if lb >= 2:
                Q.wait(E, "c_v", f"lnv_{lb - 2}")
            Q.inc(E.activation(out=dummy[:, 0:D], in_=u_stage[lb % 2][:],
                               func=AF.Square, bias=0.0, scale=1.0,
                               accum_out=stat1[:, (lb % 2):(lb % 2) + 1]),
                  "c_a", f"usq_{lb}")
            Q.wait(E, "c_v", f"lnv_{lb}")
            Q.inc(E.activation(out=stat1[:, 8 + lb % 2:9 + lb % 2],
                               in_=stat1[:, 6 + lb % 2:7 + lb % 2],
                               func=AF.Sqrt, bias=0.0, scale=1.0),
                  "c_a", f"sqv_{lb}")
            Q.wait(E, "c_v", f"lnstat_{lb}")
            Q.inc(E.activation(out=xn1[:], in_=u_stage[lb % 2][:],
                               func=AF.Identity,
                               bias=ln_nb[lb % 2][:],
                               scale=ln_sc[lb % 2][:]),
                  "c_a", f"xn_done_{lb}")
            for g in range(NGR):
                if qflat >= 2:
                    plb, pg = p0q[qflat - 2]
                    Q.wait(E, "c_p", f"xnT_{plb}_{pg}")
                Q.inc(E.activation(out=xbf_q[g % 2][:],
                                   in_=xn1[:, g * 512:(g + 1) * 512],
                                   func=AF.Copy, bias=0.0, scale=1.0),
                      "c_a", f"xbfq_{lb}_{g}")
                qflat += 1
            if own:
                for hf in range(NQ):
                    Q.wait(E, "c_v", f"sx_{tb}")
                    if hf >= 1:
                        Q.wait(E, "c_v", f"qch_{tb}_{hf - 1}")
                    Q.inc(E.activation(out=qy[:],
                                       in_=xn1[:, hf * 512:(hf + 1) * 512],
                                       func=AF.Copy, bias=0.0,
                                       scale=q_sc[:]),
                          "c_a", f"qy_{tb}_{hf}")
                Q.wait(E, "c_v", f"sd_{tb}")
                Q.inc(E.dma_start(out=s_scr.ap()[tb * 128:(tb + 1) * 128, :],
                                  in_=sd_scr[:]), "s_sd", f"sd_w_{tb}")
                if DEBUG:
                    Q.inc(E.dma_start(out=dbg_xn.ap()[tb * 128:(tb + 1) * 128, :],
                                      in_=xn1[:]), "s_dbg", f"dxn_{tb}")
                    Q.wait(E, "s_dbg", f"dxn_{tb}")
                    Q.wait(E, "c_v", f"qch_{tb}_{NQ - 1}")
                    Q.inc(E.dma_start(out=dbg_qneg.ap()[tb * 128:(tb + 1) * 128, :],
                                      in_=qneg[:]), "s_dbg", f"dqn_{tb}")
                    Q.wait(E, "s_dbg", f"dqn_{tb}")
            for k in range(NW_PER * lb, min(NW_PER * (lb + 1), NW_TOT)):
                Q.wait(E, "s_wst", f"wst_{k}")
                acc = (wsums_gu[:, k:k + 1] if k < NW_GU
                       else wsums_d[:, k - NW_GU:k - NW_GU + 1])
                nm = f"wstp_{k}" if k < NW_GU else f"wstdp_{k - NW_GU}"
                Q.inc(E.activation(out=dummy[:, 0:WCW], in_=wst[k % 2][:],
                                   func=AF.Abs, bias=0.0, scale=1.0,
                                   accum_out=acc),
                      "c_a", nm)
        for k in range(NW_PER * LB, NW_TOT):
            Q.wait(E, "s_wst", f"wst_{k}")
            acc = (wsums_gu[:, k:k + 1] if k < NW_GU
                   else wsums_d[:, k - NW_GU:k - NW_GU + 1])
            nm = f"wstp_{k}" if k < NW_GU else f"wstdp_{k - NW_GU}"
            Q.inc(E.activation(out=dummy[:, 0:WCW], in_=wst[k % 2][:],
                               func=AF.Abs, bias=0.0, scale=1.0,
                               accum_out=acc), "c_a", nm)
        for tb in range(TB):
            Q.wait(E, "s_sd", f"sd_w_{tb}")
            if tb == 0:
                Q.wait(E, "c_p", f"btp_{DC - 1}")
            if tb >= 1:
                Q.wait(E, "s_sw", f"sw_{tb - 1}")
            Q.inc(E.dma_start(out=s_stage[:],
                              in_=s_scr.ap()[tb * 128:(tb + 1) * 128, :]),
                  "s_sr", f"sr_{tb}")
            Q.wait(E, "c_v", f"s_asm_{tb}")
            Q.inc(E.dma_start(out=s_scr.ap()[tb * 128:(tb + 1) * 128, :],
                              in_=s_stage[:]), "s_sw", f"sw_{tb}")
        for ic in range(ICN):
            for mi in range(2):
                sw = swg_bc if mi == 0 else swu_bc
                for q in range(QN):
                    gqi = (ic * 2 + mi) * QN + q
                    Q.wait(E, "s_wgu", f"wgu_{gqi}")
                    if gqi == 0:
                        Q.wait(E, "c_v", "sw_ready")
                    if gqi >= 2:
                        Q.wait(E, "c_v", f"tern_gu_{gqi - 2}")
                    E.activation(out=tern1[gqi % 2][:], in_=wguf[gqi % 2][:],
                                 func=AF.Sign, bias=chalf_n[:], scale=sw[:])
                    Q.inc(E.activation(out=tern2[gqi % 2][:],
                                       in_=wguf[gqi % 2][:], func=AF.Sign,
                                       bias=chalf_p[:], scale=sw[:]),
                          "c_a", f"sgn_gu_{gqi}")
            for tb in range(TB):
                gi = ic * TB + tb
                Q.wait(E, "c_p", f"gu_mm_{gi}")
                Q.inc(E.activation(out=g_stage[gi % 2][:],
                                   in_=pb[0 + gi % 2][:], func=AF.Silu,
                                   bias=0.0, scale=cg_l[tb][:]),
                      "c_a", f"gsilu_{gi}")
                hbi = ic * NHSG + tb // HSG
                Q.wait(E, "c_v", f"h_{ic}_{tb}")
                if gi >= 2:
                    Q.wait(E, "c_v", f"hstat_{gi - 2}")
                Q.inc(E.activation(
                    out=hsq[gi % 2][:],
                    in_=h_stage[hbi % 2][:, (tb % HSG) * 512:(tb % HSG) * 512 + 512],
                    func=AF.Square, bias=0.0, scale=1.0,
                    accum_out=stat1[:, 17 + gi % 2:18 + gi % 2]),
                    "c_a", f"hsq_{gi}")
        for tb in range(TB):
            Q.wait(E, "c_v", f"dstat1_{tb}")
            E.activation(out=stat1[:, 8:9], in_=stat1[:, 6:7], func=AF.Sqrt,
                         bias=0.0, scale=1.0)
            Q.inc(E.activation(out=stat1[:, 9:10], in_=rm2[:, tb:tb + 1],
                               func=AF.Sqrt, bias=0.0, scale=1.0),
                  "c_a", f"dsq_{tb}")
            Q.wait(E, "c_v", f"dscale_{tb}")
        for kk in range(NWDR):
            rb, hhp = kk // HH, kk % HH
            if kk == 0:
                Q.wait(E, "c_p", GEND)
                Q.wait(E, "c_v", "swd_ready")
            if kk >= 2:
                Q.wait(E, "c_v", f"tern_wd_{kk - 2}")
            Q.inc(E.dma_start(out=wdf[kk % 2][:],
                              in_=wdT.ap()[rb * 128:(rb + 1) * 128,
                                           hhp * WDW:(hhp + 1) * WDW]),
                  "s_wdf", f"wdf_{kk}")
            Q.wait(E, "s_wdf", f"wdf_{kk}")
            if kk >= 1:
                Q.wait(E, "c_v", f"tern_wd_{kk - 1}")
            E.activation(out=wdsg1[:], in_=wdf[kk % 2][:], func=AF.Sign,
                         bias=chalf_n[:], scale=swd_bc[:])
            Q.inc(E.activation(out=wdsg2[:], in_=wdf[kk % 2][:], func=AF.Sign,
                               bias=chalf_p[:], scale=swd_bc[:]),
                  "c_a", f"sgn_wd_{kk}")
            Q.wait(E, "c_v", f"tern_wd_{kk}")
            Q.inc(E.dma_start(out=wd_tern.ap()[rb * 128:(rb + 1) * 128,
                                               hhp * WDW:(hhp + 1) * WDW],
                              in_=wdt_bf[kk % 2][:]), "s_wtw", f"wtw_{kk}")
        ehf = 0
        for sb in range(NSB):
            for hh in range(HH):
                for tbp in range(TBS):
                    tb = sb * TBS + tbp
                    for half in range(NHF):
                        if ehf >= 2:
                            Q.wait(E, "c_v", f"oeps_{ehf - 2}")
                        Q.inc(E.dma_start(
                            out=s_sl[ehf % 2][:],
                            in_=s_scr.ap()[tb * 128:(tb + 1) * 128,
                                           hh * HW + half * 512:
                                           hh * HW + (half + 1) * 512]),
                            "s_sxr")
                        Q.inc(E.dma_start(
                            out=x_sl[ehf % 2][:],
                            in_=u_own.ap()[tb * 128:(tb + 1) * 128,
                                           hh * HW + half * 512:
                                           hh * HW + (half + 1) * 512]),
                            "s_sxr", f"sxr_{ehf}")
                        Q.wait(E, "c_p", f"dmm_{(sb * HH + hh) * ICI + ICI - 1}")
                        Q.inc(E.activation(out=go_stage[ehf % 2][:],
                                           in_=pb[NHF * tbp + half][:],
                                           func=AF.Silu, bias=0.0,
                                           scale=cdq_l[tb][:]),
                              "c_a", f"dsilu_{ehf}")
                        Q.wait(E, "c_v", f"oeps_{ehf}")
                        Q.inc(E.dma_start(
                            out=out_d.ap()[tb * 128:(tb + 1) * 128,
                                           hh * HW + half * 512:
                                           hh * HW + (half + 1) * 512],
                            in_=o_stage[ehf % 2][:]), "s_ow", f"ow_{ehf}")
                        ehf += 1
        Q.wait_total(E, "s_ow")

    def body_vector(E, Q):
        Q.engine("vector")
        if POISON:
            Q.wait(E, "s_lbw", "poison_done")
        Q.wait(E, "c_g", "init")
        F = lambda inst, nm=None: Q.fence(E, inst, "c_v", nm)
        F(E.tensor_copy(ident_bf[:], identw[:]))
        F(E.tensor_copy(identNH_bf[:], identw[0:NH, 0:NH]), "ident_bf")
        for q in range(NQ):
            Q.wait(E, "c_p", f"dbc_mm_{q}")
            Q.inc(E.tensor_copy(dbcast[:, q * 512:(q + 1) * 512], pb[4][:]),
                  "c_v", f"dbc_ev_{q}")
        Q.wait(E, "c_a", "par_a1")
        F(E.tensor_scalar(out=PV("lre"), in0=PV("sp"), scalar1=0.0101,
                          scalar2=-1.0, op0=ALU.add, op1=ALU.mult))
        F(E.tensor_tensor(out=PV("lred"), in0=PV("lre"), in1=PV("dlt"),
                          op=ALU.mult))
        F(E.tensor_tensor(out=PV("th"), in0=PV("lim"), in1=PV("dlt"),
                          op=ALU.mult), "par_v1")
        F(E.tensor_scalar(out=PV("t1"), in0=PV("th"), scalar1=PI, scalar2=None,
                          op0=ALU.is_gt))
        F(E.tensor_scalar(out=PV("t2"), in0=PV("th"), scalar1=-PI,
                          scalar2=None, op0=ALU.is_lt))
        F(E.tensor_tensor(out=PV("t3"), in0=PV("t1"), in1=PV("t2"),
                          op=ALU.subtract))
        F(E.scalar_tensor_tensor(out=PV("thw"), in0=PV("t3"),
                                 scalar=-TWO_PI_HI, in1=PV("th"), op0=ALU.mult,
                                 op1=ALU.add))
        F(E.scalar_tensor_tensor(out=PV("thw"), in0=PV("t3"),
                                 scalar=-TWO_PI_LO, in1=PV("thw"),
                                 op0=ALU.mult, op1=ALU.add))
        F(E.tensor_scalar(out=PV("t4"), in0=PV("thw"), scalar1=-1.0,
                          scalar2=HALF_PI_HI, op0=ALU.mult, op1=ALU.add))
        F(E.tensor_scalar(out=PV("t1"), in0=PV("t4"), scalar1=PI, scalar2=None,
                          op0=ALU.is_gt))
        F(E.scalar_tensor_tensor(out=PV("thc"), in0=PV("t1"),
                                 scalar=-TWO_PI_HI, in1=PV("t4"), op0=ALU.mult,
                                 op1=ALU.add))
        F(E.tensor_scalar(out=PV("thc"), in0=PV("thc"), scalar1=HALF_PI_LO,
                          scalar2=None, op0=ALU.add), "par_v2")
        Q.wait(E, "c_a", "par_a3")
        F(E.tensor_tensor(out=PV("t1"), in0=PV("r"), in1=PV("cth"),
                          op=ALU.mult))
        F(E.tensor_tensor(out=PV("t2"), in0=PV("r"), in1=PV("sth"),
                          op=ALU.mult))
        F(E.tensor_scalar(out=PV("nre"), in0=PV("t1"), scalar1=-1.0,
                          scalar2=None, op0=ALU.add))
        F(E.tensor_tensor(out=PV("t3"), in0=PV("lre"), in1=PV("lre"),
                          op=ALU.mult))
        F(E.tensor_tensor(out=PV("t4"), in0=PV("lim"), in1=PV("lim"),
                          op=ALU.mult))
        F(E.tensor_tensor(out=PV("den"), in0=PV("t3"), in1=PV("t4"),
                          op=ALU.add))
        F(E.reciprocal(out=PV("dinv"), in_=PV("den")))
        F(E.tensor_tensor(out=PV("t3"), in0=PV("nre"), in1=PV("lre"),
                          op=ALU.mult))
        F(E.tensor_tensor(out=PV("t4"), in0=PV("t2"), in1=PV("lim"),
                          op=ALU.mult))
        F(E.tensor_tensor(out=PV("wre"), in0=PV("t3"), in1=PV("t4"),
                          op=ALU.add))
        F(E.tensor_tensor(out=PV("wre"), in0=PV("wre"), in1=PV("dinv"),
                          op=ALU.mult))
        F(E.tensor_tensor(out=PV("t3"), in0=PV("t2"), in1=PV("lre"),
                          op=ALU.mult))
        F(E.tensor_tensor(out=PV("t4"), in0=PV("nre"), in1=PV("lim"),
                          op=ALU.mult))
        F(E.tensor_tensor(out=PV("wim"), in0=PV("t3"), in1=PV("t4"),
                          op=ALU.subtract))
        F(E.tensor_tensor(out=PV("wim"), in0=PV("wim"), in1=PV("dinv"),
                          op=ALU.mult))
        F(E.tensor_scalar(out=bb1[:, 0:D], in0=bretile[:], scalar1=PV("wre"),
                          scalar2=None, op0=ALU.mult))
        F(E.tensor_scalar(out=bb2[:, 0:D], in0=bimtile[:], scalar1=PV("wim"),
                          scalar2=None, op0=ALU.mult))
        F(E.tensor_tensor(out=bb3[0:NH, :], in0=bb1[:, 0:D], in1=bb2[:, 0:D],
                          op=ALU.subtract))
        F(E.tensor_copy(bre_bf[:], bb3[0:NH, :]))
        F(E.tensor_scalar(out=bb1[:, 0:D], in0=bimtile[:], scalar1=PV("wre"),
                          scalar2=None, op0=ALU.mult))
        F(E.tensor_scalar(out=bb2[:, 0:D], in0=bretile[:], scalar1=PV("wim"),
                          scalar2=None, op0=ALU.mult))
        F(E.tensor_tensor(out=bb3[0:NH, :], in0=bb1[:, 0:D], in1=bb2[:, 0:D],
                          op=ALU.add))
        Q.inc(E.tensor_copy(bim_bf[:], bb3[0:NH, :]), "c_v", "bbar_bf")
        if L > 1:
            F(E.tensor_copy(rotc[:, 1:2], PV("cth")))
            F(E.tensor_copy(rots[:, 1:2], PV("sth")))
        n = 1
        while n < L - 1:
            k = min(n, L - 1 - n)
            cm, sm = rotc[:, n:n + 1], rots[:, n:n + 1]
            F(E.tensor_scalar(out=vv1[:, 0:k], in0=rots[:, 1:k + 1],
                              scalar1=sm, scalar2=None, op0=ALU.mult))
            F(E.scalar_tensor_tensor(out=vv2[:, 0:k], in0=rotc[:, 1:k + 1],
                                     scalar=cm, in1=vv1[:, 0:k], op0=ALU.mult,
                                     op1=ALU.subtract))
            F(E.tensor_scalar(out=vv1[:, 0:k], in0=rots[:, 1:k + 1],
                              scalar1=cm, scalar2=None, op0=ALU.mult))
            F(E.scalar_tensor_tensor(out=rots[:, n + 1:n + k + 1],
                                     in0=rotc[:, 1:k + 1], scalar=sm,
                                     in1=vv1[:, 0:k], op0=ALU.mult,
                                     op1=ALU.add))
            F(E.tensor_copy(rotc[:, n + 1:n + k + 1], vv2[:, 0:k]))
            n += k
        for dc in range(DC):
            Q.wait(E, "s_par", f"cre_{dc}")
            Q.inc(E.tensor_scalar(out=ctile_bf[:], in0=ctile[:], scalar1=2.0,
                                  scalar2=None, op0=ALU.mult),
                  "c_v", f"ctile_used_re_{dc}")
            Q.wait(E, "c_p", f"ctp_re_{dc}")
            Q.inc(E.tensor_copy(ctre_bf[:, dc * 128:(dc + 1) * 128],
                                vts[0][0:NH, 0:128]), "c_v", f"ctre_ev_{dc}")
            Q.wait(E, "s_par", f"cim_{dc}")
            Q.inc(E.tensor_scalar(out=ctile_bf[:], in0=ctile[:], scalar1=-2.0,
                                  scalar2=None, op0=ALU.mult),
                  "c_v", f"ctile_used_im_{dc}")
            Q.wait(E, "c_p", f"ctp_im_{dc}")
            Q.inc(E.tensor_copy(ctim_bf[:, dc * 128:(dc + 1) * 128],
                                vts[1][0:NH, 0:128]), "c_v", f"ctim_ev_{dc}")
        for dc in range(DC):
            Q.wait(E, "c_p", f"btp_{dc}")
            E.tensor_copy(bretT[:, dc * NH:(dc + 1) * NH], vts[0][0:128, 0:NH])
            Q.inc(E.tensor_copy(bimtT[:, dc * NH:(dc + 1) * NH],
                                vts[1][0:128, 0:NH]), "c_v", f"btp_ev_{dc}")
        for lb in range(LB):
            own = lb >= CB
            tb = lb - CB
            Q.wait(E, "s_u", f"u_{lb}")
            F(E.reduce_sum(out=stat1[:, 2 + lb % 2:3 + lb % 2],
                           in_=u_stage[lb % 2][:], axis=AX.X))
            Q.wait(E, "c_a", f"usq_{lb}")
            F(E.tensor_scalar(out=stat1[:, 4 + lb % 2:5 + lb % 2],
                              in0=stat1[:, 2 + lb % 2:3 + lb % 2],
                              scalar1=INV_D, scalar2=None, op0=ALU.mult))
            F(E.tensor_scalar(out=stat1[:, 6 + lb % 2:7 + lb % 2],
                              in0=stat1[:, lb % 2:lb % 2 + 1], scalar1=INV_D,
                              scalar2=None, op0=ALU.mult))
            F(E.tensor_tensor(out=stat1[:, 19:20],
                              in0=stat1[:, 4 + lb % 2:5 + lb % 2],
                              in1=stat1[:, 4 + lb % 2:5 + lb % 2],
                              op=ALU.mult))
            F(E.tensor_tensor(out=stat1[:, 6 + lb % 2:7 + lb % 2],
                              in0=stat1[:, 6 + lb % 2:7 + lb % 2],
                              in1=stat1[:, 19:20], op=ALU.subtract))
            F(E.tensor_scalar(out=stat1[:, 6 + lb % 2:7 + lb % 2],
                              in0=stat1[:, 6 + lb % 2:7 + lb % 2],
                              scalar1=1e-5, scalar2=None, op0=ALU.add),
              f"lnv_{lb}")
            Q.wait(E, "c_a", f"sqv_{lb}")
            F(E.reciprocal(out=ln_sc[lb % 2][:],
                           in_=stat1[:, 8 + lb % 2:9 + lb % 2]))
            F(E.scalar_tensor_tensor(out=ln_nb[lb % 2][:],
                                     in0=stat1[:, 4 + lb % 2:5 + lb % 2],
                                     scalar=-1.0, in1=ln_sc[lb % 2][:],
                                     op0=ALU.mult, op1=ALU.mult),
              f"lnstat_{lb}")
            if own:
                Q.wait(E, "c_a", f"xn_done_{lb}")
                F(E.reduce_max(out=stat1[:, 15:16], in_=xn1[:], axis=AX.X,
                               apply_absolute_value=True))
                F(E.tensor_scalar(out=stat1[:, 15:16], in0=stat1[:, 15:16],
                                  scalar1=1e-4, scalar2=None, op0=ALU.max))
                F(E.tensor_scalar(out=invsx[:, tb:tb + 1],
                                  in0=stat1[:, 15:16],
                                  scalar1=float(np.float32(1.0 / 127.0)),
                                  scalar2=None, op0=ALU.mult))
                F(E.reciprocal(out=stat1[:, 14:15], in_=stat1[:, 15:16]))
                F(E.tensor_scalar(out=q_sc[:], in0=stat1[:, 14:15],
                                  scalar1=127.0, scalar2=None, op0=ALU.mult),
                  f"sx_{tb}")
                if tb >= 1:
                    Q.wait(E, "s_sd", f"sd_w_{tb - 1}")
                Q.inc(E.tensor_tensor(out=sd_scr[:], in0=xn1[:],
                                      in1=dbcast[:], op=ALU.mult),
                      "c_v", f"sd_{tb}")
                for hf in range(NQ):
                    Q.wait(E, "c_a", f"qy_{tb}_{hf}")
                    if hf == 0 and tb > 0:
                        Q.wait(E, "c_p", f"xqT_{tb - 1}_{NGR - 1}")
                    E.tensor_copy(qi32[:], qy[:])
                    E.tensor_copy(qtf[:], qi32[:])
                    E.tensor_tensor(out=qd[:], in0=qy[:], in1=qtf[:],
                                    op=ALU.subtract)
                    E.scalar_tensor_tensor(out=qa[:], in0=qd[:], scalar=0.5,
                                           in1=qtf[:], op0=ALU.is_gt,
                                           op1=ALU.add)
                    Q.inc(E.scalar_tensor_tensor(
                        out=qneg[:, hf * 512:(hf + 1) * 512], in0=qd[:],
                        scalar=-0.5, in1=qa[:], op0=ALU.is_lt,
                        op1=ALU.subtract), "c_v", f"qch_{tb}_{hf}")
            for g in range(NGR):
                nd = min(4, DC - g * 4)
                Q.wait(E, "c_p", f"xnT_{lb}_{g}")
                dst = xnT_tile[lb % 2][:].rearrange("p (dc f) -> p dc f",
                                                    f=128)[:, g * 4:g * 4 + nd, :]
                Q.inc(E.tensor_copy(dst, vts[g % 2][:, 0:128 * nd].rearrange(
                    "p (dc f) -> p dc f", f=128)), "c_v", f"xnT_ev_{lb}_{g}")
            if own:
                for g in range(NGR):
                    nd = min(4, DC - g * 4)
                    Q.wait(E, "c_p", f"xqT_{tb}_{g}")
                    dst = x_intT[:].rearrange("p (dc f) -> p dc f", f=T)[
                        :, g * 4:g * 4 + nd, tb * 128:(tb + 1) * 128]
                    Q.inc(E.tensor_copy(dst, vts[g % 2][:, 0:128 * nd].rearrange(
                        "p (dc f) -> p dc f", f=128)), "c_v",
                        f"xqT_ev_{tb}_{g}")
            Q.wait(E, "c_p", f"bu_mm_{lb}")
            E.tensor_copy(bu[:, lb * 128:(lb + 1) * 128], pb[4][0:NH, 0:128])
            Q.inc(E.tensor_copy(bu[:, L + lb * 128:L + (lb + 1) * 128],
                                pb[5][0:NH, 0:128]), "c_v", f"bu_ev_{lb}")
        Q.wait(E, "c_a", f"wstp_{NW_GU - 1}")
        F(E.reduce_sum(out=wsum_g[:], in_=wsums_gu[:, 0:NW_GU // 2],
                       axis=AX.X))
        F(E.reduce_sum(out=wsum_u[:], in_=wsums_gu[:, NW_GU // 2:NW_GU],
                       axis=AX.X), f"wstacc_{NW_GU - 1}")
        Q.wait(E, "c_g", "atile_set")
        Q.inc(E.tensor_scalar(out=atile[:], in0=atile[:], scalar1=PV("r"),
                              scalar2=None, op0=ALU.mult), "c_v", "rot_done")
        E.tensor_tensor(out=vv1[:], in0=rotc[:], in1=bu[:, 0:L], op=ALU.mult)
        E.tensor_tensor(out=vv2[:], in0=rots[:], in1=bu[:, L:2 * L],
                        op=ALU.mult)
        E.tensor_tensor(out=vt_t[:, 0:L], in0=vv1[:], in1=vv2[:], op=ALU.add)
        E.tensor_tensor(out=vv1[:], in0=rotc[:], in1=bu[:, L:2 * L],
                        op=ALU.mult)
        E.tensor_tensor(out=vv2[:], in0=rots[:], in1=bu[:, 0:L], op=ALU.mult)
        E.tensor_tensor(out=vt_t[:, L:2 * L], in0=vv1[:], in1=vv2[:],
                        op=ALU.subtract)
        E.tensor_tensor_scan(out=zt[:, 0:L], data0=atile[:],
                             data1=vt_t[:, 0:L], initial=0.0, op0=ALU.mult,
                             op1=ALU.add)
        E.tensor_tensor_scan(out=zt[:, L:2 * L], data0=atile[:],
                             data1=vt_t[:, L:2 * L], initial=0.0,
                             op0=ALU.mult, op1=ALU.add)
        E.tensor_tensor(out=vv1[:], in0=rotc[:], in1=zt[:, 0:L], op=ALU.mult)
        E.tensor_tensor(out=vv2[:], in0=rots[:], in1=zt[:, L:2 * L],
                        op=ALU.mult)
        E.tensor_tensor(out=xs[:, 0:L], in0=vv1[:], in1=vv2[:],
                        op=ALU.subtract)
        E.tensor_tensor(out=vv1[:], in0=rotc[:], in1=zt[:, L:2 * L],
                        op=ALU.mult)
        E.tensor_tensor(out=vv2[:], in0=rots[:], in1=zt[:, 0:L], op=ALU.mult)
        E.tensor_tensor(out=xs[:, L:2 * L], in0=vv1[:], in1=vv2[:], op=ALU.add)
        E.tensor_copy(xs_bf[:, 0:T], xs[:, CTX:L])
        Q.inc(E.tensor_copy(xs_bf[:, T:2 * T], xs[:, L + CTX:2 * L]),
              "c_v", "xs_bf")
        for tb in range(TB):
            Q.wait(E, "c_p", f"y_mm_{tb}")
            Q.wait(E, "s_sr", f"sr_{tb}")
            for q in range(NQ):
                Q.inc(E.tensor_tensor(out=s_stage[:, q * 512:(q + 1) * 512],
                                      in0=pb[q % 4][:],
                                      in1=s_stage[:, q * 512:(q + 1) * 512],
                                      op=ALU.add), "c_v",
                      f"s_asm_{tb}" if q == NQ - 1 else None)
        Q.wait(E, "c_p", "wtot_mm")
        Q.inc(E.tensor_copy(sc11[0:1, 0:2], pb[5][0:1, 0:2]), "c_v",
              "wtot_ev")
        Q.wait(E, "c_p", "wtot_bc")
        for i, (mb, sb_) in enumerate([(mg_bc, swg_bc), (mu_bc, swu_bc)]):
            F(E.tensor_scalar(out=mb[:], in0=pb[4][:, i:i + 1], scalar1=INV_NW,
                              scalar2=1e-4, op0=ALU.mult, op1=ALU.max))
            F(E.reciprocal(out=sb_[:], in_=mb[:]))
        Q.inc(E.tensor_copy(dummy[0:1, 0:1], ones1[0:1, 0:1]), "c_v",
              "sw_ready")
        for tb in range(TB):
            F(E.scalar_tensor_tensor(out=cg_l[tb][:],
                                     in0=invsx[:, tb:tb + 1], scalar=-0.5,
                                     in1=mg_bc[:], op0=ALU.mult, op1=ALU.mult))
            F(E.scalar_tensor_tensor(out=cu[:, tb:tb + 1],
                                     in0=invsx[:, tb:tb + 1], scalar=-0.5,
                                     in1=mu_bc[:], op0=ALU.mult, op1=ALU.mult),
              f"cgu_{tb}" if tb == TB - 1 else None)
        for ic in range(ICN):
            for mi in range(2):
                for q in range(QN):
                    gqi = (ic * 2 + mi) * QN + q
                    Q.wait(E, "c_a", f"sgn_gu_{gqi}")
                    if ic >= 2 and mi == 0 and q == 0:
                        Q.wait(E, "c_p", f"gu_mm_{(ic - 1) * TB - 1}")
                    wdst = (wg_bf if mi == 0 else wu_bf)[ic % 2][
                        :, q * QDC * 512:(q + 1) * QDC * 512]
                    Q.inc(E.tensor_tensor(out=wdst, in0=tern1[gqi % 2][:],
                                          in1=tern2[gqi % 2][:], op=ALU.add),
                          "c_v", f"tern_gu_{gqi}")
            Q.wait(E, "c_p", f"lnwq_mm_{ic}")
            Q.inc(E.tensor_copy(lnwq_bc[ic % 2][:], pb[4][:, 0:512]), "c_v",
                  f"lnwq_cp_{ic}")
            Q.wait(E, "s_lbw", f"lbw_{ic}")
            Q.inc(E.tensor_tensor(out=lnwq_bc[ic % 2][:],
                                  in0=lnwq_bc[ic % 2][:],
                                  in1=lnwq_bc[ic % 2][:], op=ALU.mult),
                  "c_v", f"lnwq_ev_{ic}")
            for tb in range(TB):
                gi = ic * TB + tb
                hbi = ic * NHSG + tb // HSG
                Q.wait(E, "c_a", f"gsilu_{gi}")
                if hbi >= 2 and tb % HSG == 0:
                    Q.wait(E, "s_hw",
                           f"hw_{(hbi - 2) // NHSG}_{(hbi - 2) % NHSG}")
                Q.inc(E.scalar_tensor_tensor(
                    out=h_stage[hbi % 2][:, (tb % HSG) * 512:(tb % HSG) * 512 + 512],
                    in0=pb[2 + gi % 2][:], scalar=cu[:, tb:tb + 1],
                    in1=g_stage[gi % 2][:], op0=ALU.mult, op1=ALU.mult),
                    "c_v", f"h_{ic}_{tb}")
                Q.wait(E, "c_a", f"hsq_{gi}")
                F(E.tensor_tensor(out=sumsq[:, tb:tb + 1],
                                  in0=sumsq[:, tb:tb + 1],
                                  in1=stat1[:, 17 + gi % 2:18 + gi % 2],
                                  op=ALU.add))
                E.tensor_tensor(out=dummy[:, 1024:1536], in0=hsq[gi % 2][:],
                                in1=lnwq_bc[ic % 2][:], op=ALU.mult)
                F(E.reduce_max(out=stat1[:, 19:20], in_=dummy[:, 1024:1536], axis=AX.X))
                F(E.tensor_tensor(out=rm2[:, tb:tb + 1],
                                  in0=rm2[:, tb:tb + 1],
                                  in1=stat1[:, 19:20], op=ALU.max),
                  f"hstat_{gi}")
        Q.wait(E, "c_a", f"wstdp_{NW_D - 1}")
        F(E.reduce_sum(out=wsum_d[:], in_=wsums_d[:, 0:NW_D], axis=AX.X),
          f"wstdacc_{NW_D - 1}")
        Q.wait(E, "c_p", "wtotd_mm")
        Q.inc(E.tensor_copy(sc11[0:1, 2:3], pb[5][0:1, 2:3]), "c_v",
              "wtotd_ev")
        Q.wait(E, "c_p", "wtotd_bc")
        F(E.tensor_scalar(out=md_bc[:], in0=pb[4][:, 0:1], scalar1=INV_NW,
                          scalar2=1e-4, op0=ALU.mult, op1=ALU.max))
        F(E.reciprocal(out=swd_bc[:], in_=md_bc[:]))
        Q.inc(E.tensor_copy(dummy[0:1, 0:1], ones1[0:1, 0:1]), "c_v",
              "swd_ready")
        for tb in range(TB):
            Q.inc(E.tensor_scalar(out=stat1[:, 6:7], in0=sumsq[:, tb:tb + 1],
                                  scalar1=INV_I, scalar2=1e-6, op0=ALU.mult,
                                  op1=ALU.add), "c_v", f"dstat1_{tb}")
            Q.wait(E, "c_a", f"dsq_{tb}")
            F(E.reciprocal(out=stat1[:, 10:11], in_=stat1[:, 8:9]))
            F(E.tensor_tensor(out=stat1[:, 11:12], in0=stat1[:, 9:10],
                              in1=stat1[:, 10:11], op=ALU.mult))
            F(E.tensor_scalar(out=stat1[:, 11:12], in0=stat1[:, 11:12],
                              scalar1=1e-4, scalar2=None, op0=ALU.max))
            F(E.reciprocal(out=stat1[:, 12:13], in_=stat1[:, 11:12]))
            F(E.tensor_scalar(out=stat1[:, 12:13], in0=stat1[:, 12:13],
                              scalar1=127.0, scalar2=None, op0=ALU.mult))
            F(E.tensor_tensor(out=rsq[:, tb:tb + 1], in0=stat1[:, 10:11],
                              in1=stat1[:, 12:13], op=ALU.mult))
            F(E.scalar_tensor_tensor(out=cdq_l[tb][:], in0=stat1[:, 11:12],
                                     scalar=float(np.float32(-0.5 / 127.0)),
                                     in1=md_bc[:], op0=ALU.mult,
                                     op1=ALU.mult),
              f"dscale_{tb}")
        for kk in range(NWDR):
            Q.wait(E, "c_a", f"sgn_wd_{kk}")
            if kk >= 2:
                Q.wait(E, "s_wtw", f"wtw_{kk - 2}")
            Q.inc(E.tensor_tensor(out=wdt_bf[kk % 2][:], in0=wdsg1[:],
                                  in1=wdsg2[:], op=ALU.add),
                  "c_v", f"tern_wd_{kk}")
        ehf = 0
        for sb in range(NSB):
            nch = TBS * HCH
            for ch in range(nch):
                tbp, c = ch // HCH, ch % HCH
                tb = sb * TBS + tbp
                gi = sb * nch + ch
                Q.wait(E, "s_hrd", f"hrd_{gi}")
                Q.wait(E, "s_lnr", f"lnr_{gi}")
                Q.wait(E, "c_v", f"dscale_{tb}")
                Q.inc(E.scalar_tensor_tensor(
                    out=hy[gi % 2][:], in0=hrd[gi % 2][:],
                    scalar=rsq[:, tb:tb + 1], in1=lnr_sl[gi % 2][:],
                    op0=ALU.mult, op1=ALU.mult), "c_v", f"hy_{gi}")
                Q.wait(E, "c_g", f"hcvt_{gi}")
                if gi >= 2:
                    Q.wait(E, "c_p", f"qhT_{gi - 2}_{HGR - 1}")
                E.tensor_tensor(out=hqd[:], in0=hy[gi % 2][:],
                                in1=hq_tf[gi % 2][:], op=ALU.subtract)
                E.scalar_tensor_tensor(out=hqa[:], in0=hqd[:], scalar=0.5,
                                       in1=hq_tf[gi % 2][:], op0=ALU.is_gt,
                                       op1=ALU.add)
                Q.inc(E.scalar_tensor_tensor(out=qh_tok[gi % 2][:],
                                             in0=hqd[:], scalar=-0.5,
                                             in1=hqa[:], op0=ALU.is_lt,
                                             op1=ALU.subtract),
                      "c_v", f"hq_done_{gi}")
                for g in range(HGR):
                    Q.wait(E, "c_p", f"qhT_{gi}_{g}")
                    dst = qhT[:].rearrange("p (ii f) -> p ii f",
                                           f=TBS * 128)[
                        :, (c * HCW // 128) + g * 4:(c * HCW // 128) + g * 4 + 4,
                        tbp * 128:(tbp + 1) * 128]
                    Q.inc(E.tensor_copy(dst, vts[g % 2][:, 0:512].rearrange(
                        "p (ii f) -> p ii f", f=128)),
                        "c_v", f"qhT_ev_{gi}_{g}")
            for hh in range(HH):
                for tbp in range(TBS):
                    for half in range(NHF):
                        Q.wait(E, "c_a", f"dsilu_{ehf}")
                        Q.wait(E, "s_sxr", f"sxr_{ehf}")
                        if ehf >= 2:
                            Q.wait(E, "s_ow", f"ow_{ehf - 2}")
                        E.tensor_tensor(out=o_stage[ehf % 2][:],
                                        in0=go_stage[ehf % 2][:],
                                        in1=s_sl[ehf % 2][:], op=ALU.mult)
                        Q.inc(E.tensor_tensor(out=o_stage[ehf % 2][:],
                                              in0=o_stage[ehf % 2][:],
                                              in1=x_sl[ehf % 2][:],
                                              op=ALU.add),
                              "c_v", f"oeps_{ehf}")
                        ehf += 1

    def body_tensor(E, Q):
        Q.engine("tensor")
        if POISON:
            Q.wait(E, "s_lbw", "poison_done")
        Q.wait(E, "c_v", "ident_bf")
        Q.wait(E, "s_par", "par_core")
        for q in range(NQ):
            if q >= 1:
                Q.wait(E, "c_v", f"dbc_ev_{q - 1}")
            Q.inc(E.matmul(pb[4][0:128, 0:512], ones1[:],
                           dbcast[0:1, q * 512:(q + 1) * 512], start=True,
                           stop=True), "c_p", f"dbc_mm_{q}")
        Q.wait(E, "c_v", f"dbc_ev_{NQ - 1}")
        for dc in range(DC):
            Q.wait(E, "c_v", f"ctile_used_re_{dc}")
            if dc > 0:
                Q.wait(E, "c_v", f"ctre_ev_{dc - 1}")
            Q.inc(E.transpose(vts[0][0:NH, 0:128], ctile_bf[:], ident_bf[:]),
                  "c_p", f"ctp_re_{dc}")
            Q.wait(E, "c_v", f"ctile_used_im_{dc}")
            if dc > 0:
                Q.wait(E, "c_v", f"ctim_ev_{dc - 1}")
            Q.inc(E.transpose(vts[1][0:NH, 0:128], ctile_bf[:], ident_bf[:]),
                  "c_p", f"ctp_im_{dc}")
        Q.wait(E, "c_v", f"ctre_ev_{DC - 1}")
        Q.wait(E, "c_v", f"ctim_ev_{DC - 1}")
        Q.wait(E, "c_v", "bbar_bf")
        for dc in range(DC):
            if dc > 0:
                Q.wait(E, "c_v", f"btp_ev_{dc - 1}")
            E.transpose(vts[0][0:128, 0:NH], bre_bf[:, dc * 128:(dc + 1) * 128],
                        identNH_bf[:])
            Q.inc(E.transpose(vts[1][0:128, 0:NH],
                              bim_bf[:, dc * 128:(dc + 1) * 128],
                              identNH_bf[:]), "c_p", f"btp_{dc}")
        Q.wait(E, "c_v", f"btp_ev_{DC - 1}")
        p0i = 0
        for lb in range(LB):
            own = lb >= CB
            tb = lb - CB
            for g in range(NGR):
                nd = min(4, DC - g * 4)
                Q.wait(E, "c_a", f"xbfq_{lb}_{g}")
                gd = p0guard(p0i)
                if gd is not None:
                    Q.wait(E, "c_v", gd)
                for j in range(nd):
                    inst = E.transpose(vts[g % 2][:, j * 128:(j + 1) * 128],
                                       xbf_q[g % 2][:, j * 128:(j + 1) * 128],
                                       ident_bf[:])
                    if j == nd - 1:
                        Q.inc(inst, "c_p", f"xnT_{lb}_{g}")
                p0i += 1
            Q.wait(E, "c_v", f"xnT_ev_{lb}_{NGR - 1}")
            if lb >= 1:
                Q.wait(E, "c_v", f"bu_ev_{lb - 1}")
            for dc in range(DC):
                E.matmul(pb[4][0:NH, 0:128], bretT[:, dc * NH:(dc + 1) * NH],
                         xnT_tile[lb % 2][:, dc * 128:(dc + 1) * 128],
                         start=(dc == 0), stop=(dc == DC - 1))
                inst = E.matmul(pb[5][0:NH, 0:128],
                                bimtT[:, dc * NH:(dc + 1) * NH],
                                xnT_tile[lb % 2][:, dc * 128:(dc + 1) * 128],
                                start=(dc == 0), stop=(dc == DC - 1))
                if dc == DC - 1:
                    Q.inc(inst, "c_p", f"bu_mm_{lb}")
            if own:
                for g in range(NGR):
                    nd = min(4, DC - g * 4)
                    Q.wait(E, "c_v", f"qch_{tb}_{g}")
                    gd = p0guard(p0i)
                    if gd is not None:
                        Q.wait(E, "c_v", gd)
                    for j in range(nd):
                        inst = E.transpose(
                            vts[g % 2][:, j * 128:(j + 1) * 128],
                            qneg[:, (g * 4 + j) * 128:(g * 4 + j + 1) * 128],
                            ident_bf[:])
                        if j == nd - 1:
                            Q.inc(inst, "c_p", f"xqT_{tb}_{g}")
                    p0i += 1
        Q.wait(E, "c_v", "xs_bf")
        for tb in range(TB):
            if tb >= 1:
                Q.wait(E, "c_v", f"s_asm_{tb - 1}")
            lhs_re = xs_bf[:, tb * 128:(tb + 1) * 128]
            lhs_im = xs_bf[:, T + tb * 128:T + (tb + 1) * 128]
            for q in range(NQ):
                E.matmul(pb[q % 4][:], lhs_re,
                         ctre_bf[:, q * 512:(q + 1) * 512], start=True,
                         stop=False)
                inst = E.matmul(pb[q % 4][:], lhs_im,
                                ctim_bf[:, q * 512:(q + 1) * 512],
                                start=False, stop=True)
                if q == NQ - 1:
                    Q.inc(inst, "c_p", f"y_mm_{tb}")
        Q.wait(E, "c_v", f"wstacc_{NW_GU - 1}")
        Q.wait(E, "c_v", f"s_asm_{TB - 1}")
        for i, acc in enumerate([wsum_g, wsum_u]):
            inst = E.matmul(pb[5][0:1, i:i + 1], acc[:], ones128[:],
                            start=True, stop=True)
            if i == 1:
                Q.inc(inst, "c_p", "wtot_mm")
        Q.wait(E, "c_v", "wtot_ev")
        Q.inc(E.matmul(pb[4][0:128, 0:2], ones1[:], sc11[0:1, 0:2],
                       start=True, stop=True), "c_p", "wtot_bc")
        for ic in range(ICN):
            Q.wait(E, "c_v", "sw_ready")
            Q.wait(E, "s_lnw", f"lnw_{ic}")
            if ic >= 1:
                Q.wait(E, "c_v", f"lnwq_cp_{ic - 1}")
            Q.inc(E.matmul(pb[4][0:128, 0:512], ones1[:], lnw_sl[ic % 2][:],
                           start=True, stop=True), "c_p", f"lnwq_mm_{ic}")
            for mi in range(2):
                for q in range(QN):
                    Q.wait(E, "c_v", f"tern_gu_{(ic * 2 + mi) * QN + q}")
            Q.wait(E, "c_v", f"cgu_{TB - 1}")
            for tb in range(TB):
                gi = ic * TB + tb
                if gi >= 2:
                    Q.wait(E, "c_a", f"gsilu_{gi - 2}")
                    Q.wait(E, "c_v", f"h_{(gi - 2) // TB}_{(gi - 2) % TB}")
                for dc in range(DC):
                    E.matmul(pb[0 + gi % 2][:],
                             x_intT[:, dc * T + tb * 128: dc * T + (tb + 1) * 128],
                             wg_bf[ic % 2][:, dc * 512:(dc + 1) * 512],
                             start=(dc == 0), stop=(dc == DC - 1))
                for dc in range(DC):
                    inst = E.matmul(pb[2 + gi % 2][:],
                                    x_intT[:, dc * T + tb * 128: dc * T + (tb + 1) * 128],
                                    wu_bf[ic % 2][:, dc * 512:(dc + 1) * 512],
                                    start=(dc == 0), stop=(dc == DC - 1))
                    if dc == DC - 1:
                        Q.inc(inst, "c_p", f"gu_mm_{gi}")
        Q.wait(E, "c_v", f"wstdacc_{NW_D - 1}")
        Q.inc(E.matmul(pb[5][0:1, 2:3], wsum_d[:], ones128[:], start=True,
                       stop=True), "c_p", "wtotd_mm")
        Q.wait(E, "c_v", "wtotd_ev")
        Q.wait(E, "c_v", f"lnwq_ev_{ICN - 1}")
        Q.inc(E.matmul(pb[4][0:128, 0:1], ones1[:], sc11[0:1, 2:3],
                       start=True, stop=True), "c_p", "wtotd_bc")
        for sb in range(NSB):
            nch = TBS * HCH
            for ch in range(nch):
                gi = sb * nch + ch
                Q.wait(E, "c_v", f"hq_done_{gi}")
                if sb > 0 and ch == 0:
                    Q.wait(E, "c_v", f"oeps_{sb * HH * TBS * NHF - 1}")
                for g in range(HGR):
                    if g >= 2:
                        Q.wait(E, "c_v", f"qhT_ev_{gi}_{g - 2}")
                    elif gi > 0:
                        Q.wait(E, "c_v", f"qhT_ev_{gi - 1}_{HGR - 2 + g}")
                    for j in range(4):
                        inst = E.transpose(
                            vts[g % 2][:, j * 128:(j + 1) * 128],
                            qh_tok[gi % 2][:, (g * 4 + j) * 128:(g * 4 + j + 1) * 128],
                            ident_bf[:])
                        if j == 3:
                            Q.inc(inst, "c_p", f"qhT_{gi}_{g}")
            for hh in range(HH):
                Q.wait(E, "c_v", f"qhT_ev_{sb * nch + nch - 1}_{HGR - 1}")
                if sb == 0 and hh == 0:
                    Q.wait(E, "c_a", f"gsilu_{ICN * TB - 1}")
                if sb * HH + hh >= 1:
                    Q.wait(E, "c_v",
                           f"oeps_{(sb * HH + hh) * TBS * NHF - 1}")
                for icI in range(ICI):
                    wi = (sb * HH + hh) * ICI + icI
                    Q.wait(E, "s_wdb", f"wdb_{wi}")
                    last = None
                    for tbp in range(TBS):
                        for half in range(NHF):
                            last = E.matmul(
                                pb[NHF * tbp + half][:],
                                qhT[:, icI * TBS * 128 + tbp * 128: icI * TBS * 128 + (tbp + 1) * 128],
                                wdb[wi % NWDB][:, half * 512:(half + 1) * 512],
                                start=(icI == 0), stop=(icI == ICI - 1))
                    Q.inc(last, "c_p", f"dmm_{wi}")

    bodies = [body_sync, body_gpsimd, body_scalar, body_vector, body_tensor]
    dry = P(tracks, emit=False)
    mock = _MockEng()
    for b in bodies:
        b(mock, dry)
    dry.cp.update({"TOTAL_" + t.name: t.n for t in tracks})
    _check_deadlock(dry)

    for t in tracks:
        t.n = 0
    emitP = P(tracks, emit=True, frozen=dry.cp)
    with contextlib.ExitStack() as es:
        for t in tracks:
            t.sem = es.enter_context(nc.semaphore(t.name))
        with nc.Block() as block:
            block.sync(lambda e: body_sync(e, emitP))
            block.gpsimd(lambda e: body_gpsimd(e, emitP))
            block.scalar(lambda e: body_scalar(e, emitP))
            block.vector(lambda e: body_vector(e, emitP))
            block.tensor(lambda e: body_tensor(e, emitP))

    _PROG_CACHE[key] = nc
    return nc


def make_in_maps(cfg, u, wg, wu, wd, lnw, bre, bim, cre, cim, dv, lunc, lim,
                 logd, n_cores=8):
    D, T, CTX, NH = cfg["D"], cfg["T"], cfg["CTX"], cfg["NH"]
    shared = dict(
        wgT=np.ascontiguousarray(np.asarray(wg, np.float32).T),
        wuT=np.ascontiguousarray(np.asarray(wu, np.float32).T),
        wdT=np.ascontiguousarray(np.asarray(wd, np.float32).T),
        lnw_d=np.ascontiguousarray(np.asarray(lnw, np.float32).reshape(1, -1)),
        bre_d=np.ascontiguousarray(np.asarray(bre, np.float32)),
        bim_d=np.ascontiguousarray(np.asarray(bim, np.float32)),
        cre_d=np.ascontiguousarray(np.asarray(cre, np.float32)),
        cim_d=np.ascontiguousarray(np.asarray(cim, np.float32)),
        dvec_d=np.ascontiguousarray(np.asarray(dv, np.float32).reshape(1, -1)),
        lunc_d=np.ascontiguousarray(
            np.asarray(lunc, np.float32).reshape(-1, 1)),
        lim_d=np.ascontiguousarray(np.asarray(lim, np.float32).reshape(-1, 1)),
        logd_d=np.ascontiguousarray(
            np.asarray(logd, np.float32).reshape(-1, 1)[:NH]),
    )
    u = np.asarray(u, np.float32)
    per_seq = u.shape[1] // T
    in_maps = []
    for c in range(n_cores):
        b, h = c // per_seq, c % per_seq
        m = dict(shared)
        m["u_own"] = np.ascontiguousarray(u[b, h * T:(h + 1) * T, :])
        m["u_ctx"] = (np.zeros((CTX, D), np.float32) if h == 0 else
                      np.ascontiguousarray(u[b, h * T - CTX:h * T, :]))
        in_maps.append(m)
    return in_maps


def kernel(L0_attn_u, L0_attn_Lambda_unconstrained_re, L0_attn_Lambda_im,
           L0_attn_B_re, L0_attn_B_im, L0_attn_C_re, L0_attn_C_im, L0_attn_D,
           L0_attn_log_Delta, L0_ffn_w_gate, L0_ffn_w_up, L0_ffn_w_down,
           L0_ffn_ln_weight):
    cfg = FULL_CFG
    D, T = cfg["D"], cfg["T"]
    u = np.asarray(L0_attn_u, np.float32)
    B, Lseq, _ = u.shape
    nc = build_program(cfg)
    in_maps = make_in_maps(cfg, u, L0_ffn_w_gate, L0_ffn_w_up, L0_ffn_w_down,
                           L0_ffn_ln_weight, L0_attn_B_re, L0_attn_B_im,
                           L0_attn_C_re, L0_attn_C_im, L0_attn_D,
                           L0_attn_Lambda_unconstrained_re, L0_attn_Lambda_im,
                           L0_attn_log_Delta, n_cores=8)
    res = run_bass_kernel_spmd(nc, in_maps, core_ids=list(range(8)))
    out = np.empty((B, Lseq, D), np.float32)
    per_seq = Lseq // T
    for c in range(8):
        b, h = c // per_seq, c % per_seq
        out[b, h * T:(h + 1) * T, :] = res.results[c]["out"]
    return out


# version 05
# speedup vs baseline: 1.0613x; 1.0613x over previous
import numpy as np
import contextlib
import concourse.bass as bass
import concourse.mybir as mybir
from concourse.bass_utils import run_bass_kernel_spmd

F32 = mybir.dt.float32
BF16 = mybir.dt.bfloat16
I32 = mybir.dt.int32
AF = mybir.ActivationFunctionType
ALU = mybir.AluOpType
AX = mybir.AxisListType

FULL_CFG = dict(D=2048, I=8192, T=1024, CTX=1024, NH=32)

PI = float(np.float32(np.pi))
TWO_PI_HI = float(np.float32(2.0 * np.pi))
TWO_PI_LO = float(np.float32(2.0 * np.pi - float(np.float32(2.0 * np.pi))))
HALF_PI_HI = float(np.float32(0.5 * np.pi))
HALF_PI_LO = float(np.float32(0.5 * np.pi - float(np.float32(0.5 * np.pi))))


class _MockInst:
    def then_inc(self, *a, **k):
        return self


class _MockEng:
    def __getattr__(self, name):
        def f(*a, **k):
            return _MockInst()
        return f


class Track:
    def __init__(self, name, dma):
        self.name = name
        self.dma = dma
        self.sem = None
        self.n = 0


class P:
    def __init__(self, tracks, emit, frozen=None):
        self.tracks = {t.name: t for t in tracks}
        self.emit = emit
        self.cp = {}
        self.frozen = frozen or {}
        self.events = {}
        self.cur = None

    def engine(self, name):
        self.cur = name
        self.events.setdefault(name, [])

    def inc(self, inst, tname, name=None, k=None):
        t = self.tracks[tname]
        kk = k if k is not None else (16 if t.dma else 1)
        t.n += kk
        if name is not None:
            assert name not in self.cp, f"dup checkpoint {name}"
            self.cp[name] = t.n
        if self.emit:
            inst.then_inc(t.sem, kk)
        else:
            self.events[self.cur].append(("inc", tname, kk))
        return t.n

    def wait(self, eng, tname, name):
        if self.emit:
            eng.wait_ge(self.tracks[tname].sem, self.frozen[name])
        else:
            self.events[self.cur].append(("wait", tname, name))

    def wait_total(self, eng, tname):
        if self.emit:
            eng.wait_ge(self.tracks[tname].sem, self.frozen["TOTAL_" + tname])
        else:
            self.events[self.cur].append(("wait_total", tname, None))

    def fence(self, eng, inst, tname, name=None):
        v = self.inc(inst, tname, name)
        if self.emit:
            eng.wait_ge(self.tracks[tname].sem, v)
        else:
            self.events[self.cur].append(("wait_val", tname, v))


def _check_deadlock(P0):
    vals = {t: 0 for t in P0.tracks}
    pcs = {e: 0 for e in P0.events}
    progress = True
    while progress:
        progress = False
        for e, evs in P0.events.items():
            while pcs[e] < len(evs):
                kind, tname, x = evs[pcs[e]]
                if kind == "inc":
                    vals[tname] += x
                elif kind == "wait":
                    if vals[tname] < P0.cp[x]:
                        break
                elif kind == "wait_val":
                    if vals[tname] < x:
                        break
                else:
                    if vals[tname] < P0.cp["TOTAL_" + tname]:
                        break
                pcs[e] += 1
                progress = True
    stuck = {}
    for e, evs in P0.events.items():
        if pcs[e] < len(evs):
            kind, tname, x = evs[pcs[e]]
            need = P0.cp[x] if kind == "wait" else P0.cp.get("TOTAL_" + tname)
            stuck[e] = (f"pc {pcs[e]}/{len(evs)}", kind, tname, x,
                        "need", need, "have", vals[tname])
    if stuck:
        raise RuntimeError(f"build-time deadlock: {stuck}")


class Arena:
    def __init__(self, nc, base_addr, size, tag):
        self.nc = nc
        self.base = base_addr
        self.size = size
        self.cur = 0
        self.tag = tag
        self.offs = {}

    def _mk(self, name, shape, dtype, addr):
        h = self.nc.virtual_sbuf_tensor(name, list(shape), dtype)
        self.nc._tensor(h.name, list(h.shape), h.dtype, type="SB")
        mloc = self.nc.lookup_mloc(h)
        mloc.allocated = True
        mloc.addr = addr
        return h

    def T(self, name, shape, dtype=F32):
        per = int(np.prod(shape[1:])) * mybir.dt.size(dtype)
        per = (per + 31) // 32 * 32
        assert self.cur + per <= self.size, \
            f"arena {self.tag} overflow at {name}: {self.cur + per} > {self.size}"
        h = self._mk(name, shape, dtype, self.base + self.cur)
        self.offs[name] = self.cur
        self.cur += per
        return h

    def at(self, off, name, shape, dtype=F32):
        per = int(np.prod(shape[1:])) * mybir.dt.size(dtype)
        assert off + per <= self.size
        h = self._mk(name, shape, dtype, self.base + off)
        self.offs[name] = off
        return h


_PROG_CACHE = {}


def build_program(cfg):
    key = tuple(sorted(cfg.items()))
    if key in _PROG_CACHE:
        return _PROG_CACHE[key]

    D, I, T, CTX, NH = cfg["D"], cfg["I"], cfg["T"], cfg["CTX"], cfg["NH"]
    L = T + CTX
    DC = D // 128
    TB = T // 128
    CB = CTX // 128
    LB = L // 128
    ICN = I // 512
    ICI = I // 128
    NQ = D // 512
    NGR = (DC + 3) // 4
    assert NGR == NQ
    QN = 2 * max(1, DC // 8)
    QDC = DC // QN
    assert QDC * QN == DC
    NSB = max(1, TB // 4)
    TBS = TB // NSB
    HH = max(1, D // 1024)
    HW = D // HH
    NHF = HW // 512
    HCW = min(I, 1024)
    HCH = I // HCW
    HGR = HCW // 512
    WCW = min(I, 512)
    NW_GU = (D // 128) * (I // WCW) * 2
    WDW = HW
    NWDR = ICI * HH
    WSD = min(D, 512)
    NW_D = (I // 128) * (D // WSD)
    INV_D = float(np.float32(1.0 / D))
    INV_I = float(np.float32(1.0 / I))
    INV_NW = float(np.float32(1.0 / (D * I)))
    HSG = min(4, TB)
    NHSG = TB // HSG

    nc = bass.Bass("TRN2", target_bir_lowering=False, debug=False,
                   detect_race_conditions=False)

    u_own = nc.dram_tensor("u_own", [T, D], F32, kind="ExternalInput")
    u_ctx = nc.dram_tensor("u_ctx", [CTX, D], F32, kind="ExternalInput")
    wgT = nc.dram_tensor("wgT", [D, I], F32, kind="ExternalInput")
    wuT = nc.dram_tensor("wuT", [D, I], F32, kind="ExternalInput")
    wdT = nc.dram_tensor("wdT", [I, D], F32, kind="ExternalInput")
    lnw_d = nc.dram_tensor("lnw_d", [1, I], F32, kind="ExternalInput")
    bre_d = nc.dram_tensor("bre_d", [NH, D], F32, kind="ExternalInput")
    bim_d = nc.dram_tensor("bim_d", [NH, D], F32, kind="ExternalInput")
    cre_d = nc.dram_tensor("cre_d", [D, NH], F32, kind="ExternalInput")
    cim_d = nc.dram_tensor("cim_d", [D, NH], F32, kind="ExternalInput")
    dvec_d = nc.dram_tensor("dvec_d", [1, D], F32, kind="ExternalInput")
    lunc_d = nc.dram_tensor("lunc_d", [NH, 1], F32, kind="ExternalInput")
    lim_d = nc.dram_tensor("lim_d", [NH, 1], F32, kind="ExternalInput")
    logd_d = nc.dram_tensor("logd_d", [NH, 1], F32, kind="ExternalInput")
    out_d = nc.dram_tensor("out", [T, D], F32, kind="ExternalOutput")
    h_scr = nc.dram_tensor("h_scr", [T, I], F32, kind="Internal")
    s_scr = nc.dram_tensor("s_scr", [T, D], F32, kind="Internal")
    wd_tern = nc.dram_tensor("wd_tern", [I, D], BF16, kind="Internal")
    lnw_scr = nc.dram_tensor("lnw_scr", [128, I], F32, kind="Internal")
    DEBUG = bool(cfg.get("DEBUG", 0))
    DBGMD = bool(cfg.get("DBGMD", 0))
    POISON = bool(cfg.get("POISON", 0))
    if DEBUG:
        dbg_xn = nc.dram_tensor("dbg_xn", [T, D], F32, kind="ExternalOutput")
        dbg_qneg = nc.dram_tensor("dbg_qneg", [T, D], BF16,
                                  kind="ExternalOutput")
        dbg_xint = nc.dram_tensor("dbg_xint", [128, DC * T], BF16,
                                  kind="ExternalOutput")
        dbg_s = nc.dram_tensor("dbg_s", [T, D], F32, kind="ExternalOutput")
        dbg_h = nc.dram_tensor("dbg_h", [T, I], F32, kind="ExternalOutput")
        dbg_wdt = nc.dram_tensor("dbg_wdt", [I, D], BF16,
                                 kind="ExternalOutput")
        dbg_lnw = nc.dram_tensor("dbg_lnw", [128, I], F32,
                                 kind="ExternalOutput")
        dbg_stats = nc.dram_tensor("dbg_stats", [128, 8 * TB + 40], F32,
                                   kind="ExternalOutput")
        dbg_ssm = nc.dram_tensor("dbg_ssm", [NH, 7 * L], F32,
                                 kind="ExternalOutput")
        dbg_par = nc.dram_tensor("dbg_par", [NH, 26], F32,
                                 kind="ExternalOutput")
    if DEBUG or DBGMD:
        dbg_wsd = nc.dram_tensor("dbg_wsd", [128, 260], F32,
                                 kind="ExternalOutput")

    A = lambda n, s, d=F32: nc.alloc_sbuf_tensor(n, s, d)
    ones1 = A("ones1", [1, 128])
    ones128 = A("ones128", [128, 1])
    ident_bf = A("ident_bf", [128, 128], BF16)
    identw = A("identw", [128, 128])
    identNH_bf = A("identNH_bf", [NH, NH], BF16)
    par = A("par", [NH, 26])
    pc = {k: i for i, k in enumerate(
        ["lunc", "lim", "logd", "e1", "sp", "lre", "dlt", "lred", "th",
         "r", "thw", "cth", "thc", "sth", "wre", "wim", "den", "dinv",
         "nre", "t1", "t2", "t3", "t4"])}
    PV = lambda k: par[:, pc[k]:pc[k] + 1]
    invsx = A("invsx", [128, TB])
    sumsq = A("sumsq", [128, TB])
    rm2 = A("rm2", [128, TB])
    cg_l = [A(f"cg{i}", [128, 1]) for i in range(TB)]
    cu = A("cu", [128, TB])
    rsq = A("rsq", [128, TB])
    cdq_l = [A(f"cdq{i}", [128, 1]) for i in range(TB)]
    stat1 = A("stat1", [128, 20])
    wsum_g = A("wsum_g", [128, 1])
    wsum_u = A("wsum_u", [128, 1])
    wsum_d = A("wsum_d", [128, 1])
    wsums_gu = A("wsums_gu", [128, 1024])
    wsums_d = A("wsums_d", [128, 256])
    swg_bc = A("swg_bc", [128, 1])
    swu_bc = A("swu_bc", [128, 1])
    swd_bc = A("swd_bc", [128, 1])
    mg_bc = A("mg_bc", [128, 1])
    mu_bc = A("mu_bc", [128, 1])
    md_bc = A("md_bc", [128, 1])
    sc11 = A("sc11", [1, 8])
    chalf_n = A("chalf_n", [128, 1])
    chalf_p = A("chalf_p", [128, 1])
    ln_sc = [A(f"ln_sc{i}", [128, 1]) for i in range(2)]
    ln_nb = [A(f"ln_nb{i}", [128, 1]) for i in range(2)]
    q_sc = A("q_sc", [128, 1])
    lnw_sl = [A(f"lnw_sl{i}", [1, 512]) for i in range(2)]
    dummy = A("dummy_sb", [128, 2048])
    bretT = A("bretT", [128, DC * NH], BF16)
    bimtT = A("bimtT", [128, DC * NH], BF16)
    ctre_bf = A("ctre_bf", [NH, D], BF16)
    ctim_bf = A("ctim_bf", [NH, D], BF16)
    ctile = A("ctile", [128, NH])
    ctile_bf = A("ctile_bf", [128, NH], BF16)

    pb = [nc.alloc_psum_tensor(f"pb{i}", [128, 512], F32) for i in range(8)]
    vts = [pb[6][:].bitcast(BF16), pb[7][:].bitcast(BF16)]

    remaining = nc.sbuf_bytes_remaining - 512
    scratch = nc.alloc_sbuf_tensor("scratch", [128, remaining], mybir.dt.uint8)
    sbase = nc.lookup_mloc(scratch).addr
    XZ = Arena(nc, sbase, remaining, "XZ")
    x_intT = XZ.T("x_intT", [128, DC * T], BF16)
    ph_base = sbase + XZ.cur
    ph_size = remaining - XZ.cur
    a0 = Arena(nc, ph_base, ph_size, "P0")
    aG = Arena(nc, ph_base, ph_size, "G")
    aD = Arena(nc, sbase, remaining, "D")
    if POISON:
        aPZ = Arena(nc, sbase, remaining, "PZ")
        PZC = remaining // 4
        pzf = aPZ.at(0, "pzf", [128, PZC])
        pzb = aPZ.at(0, "pzb", [128, remaining // 2], BF16)

    u_stage = [a0.T(f"u_stage{i}", [128, D]) for i in range(2)]
    xn1 = a0.T("xn1", [128, D])
    xbf_q = [a0.T(f"xbf_q{i}", [128, 512], BF16) for i in range(2)]
    xnT_tile = [a0.T(f"xnT_tile{i}", [128, D], BF16) for i in range(2)]
    qy = a0.T("qy", [128, 512])
    qi32 = a0.T("qi32", [128, 512], I32)
    qtf = a0.T("qtf", [128, 512], BF16)
    qd = a0.T("qd", [128, 512])
    qa = a0.T("qa", [128, 512], BF16)
    qneg = a0.T("qneg", [128, D], BF16)
    dbcast = a0.T("dbcast", [128, D])
    rotc = a0.T("rotc", [NH, L])
    rots = a0.T("rots", [NH, L])
    bu = a0.T("bu", [NH, 2 * L])
    vt_t = a0.T("vt_t", [NH, 2 * L])
    vv1 = a0.T("vv1", [NH, L])
    vv2 = a0.T("vv2", [NH, L])
    xs_bf = a0.T("xs_bf", [NH, 2 * T], BF16)
    s_stage = a0.T("s_stage", [128, D])
    sd_scr = a0.T("sd_scr", [128, D])
    wst = [a0.T(f"wst{i}", [128, WCW]) for i in range(3)]
    bretile = a0.at(a0.offs["bu"], "bretile", [NH, D])
    bimtile = a0.at(a0.offs["bu"] + 4 * L, "bimtile", [NH, D])
    assert a0.offs["qa"] + 1024 - a0.offs["qy"] >= 4 * L
    atile = a0.at(a0.offs["qy"], "atile", [NH, L])
    bre_bf = a0.at(a0.offs["s_stage"], "bre_bf", [NH, D], BF16)
    bim_bf = a0.at(a0.offs["s_stage"] + 2 * D, "bim_bf", [NH, D], BF16)
    bb3 = sd_scr
    bb1, bb2 = vv1, vv2
    zt = bu
    xs = vt_t

    wguf = [aG.T(f"wguf{i}", [128, QDC * 512]) for i in range(2)]
    tern1 = [aG.T(f"tern1_{i}", [128, QDC * 512]) for i in range(2)]
    tern2 = [aG.T(f"tern2_{i}", [128, QDC * 512]) for i in range(2)]
    wg_bf = [aG.T(f"wg_bf{i}", [128, DC * 512], BF16) for i in range(2)]
    wu_bf = [aG.T(f"wu_bf{i}", [128, DC * 512], BF16) for i in range(2)]
    h_stage = [aG.T(f"h_stage{i}", [128, HSG * 512]) for i in range(2)]
    g_stage = [aG.T(f"g_stage{i}", [128, 512]) for i in range(2)]
    hsq = [aG.T(f"hsq{i}", [128, 512]) for i in range(2)]
    hlw = None
    lnwq_bc = [aG.T(f"lnwq_bc{i}", [128, 512]) for i in range(2)]

    qhT = aD.T("qhT", [128, ICI * TBS * 128], BF16)
    lnr_sl = [aD.T(f"lnr_sl{i}", [128, HCW]) for i in range(2)]
    hrd = [aD.T(f"hrd{i}", [128, HCW]) for i in range(2)]
    hy = [aD.T(f"hy{i}", [128, HCW]) for i in range(2)]
    hq_i32 = [aD.T(f"hq_i32{i}", [128, HCW], I32) for i in range(2)]
    hq_tf = [aD.T(f"hq_tf{i}", [128, HCW], BF16) for i in range(2)]
    hqd = aD.T("hqd", [128, HCW])
    hqa = aD.T("hqa", [128, HCW], BF16)
    qh_tok = [aD.T(f"qh_tok{i}", [128, HCW], BF16) for i in range(2)]
    NWDB = 4
    wdb = [aD.T(f"wdb{i}", [128, HW], BF16) for i in range(NWDB)]
    wdf = [aD.T(f"wdf{i}", [128, WDW]) for i in range(2)]
    wdsg1 = aD.T("wdsg1", [128, WDW])
    wdsg2 = aD.T("wdsg2", [128, WDW])
    wdt_bf = [aD.T(f"wdt_bf{i}", [128, WDW], BF16) for i in range(2)]
    s_sl = [aD.T(f"s_sl{i}", [128, 512]) for i in range(2)]
    x_sl = [aD.T(f"x_sl{i}", [128, 512]) for i in range(2)]
    go_stage = [aD.T(f"go_stage{i}", [128, 512]) for i in range(2)]
    o_stage = [aD.T(f"o_stage{i}", [128, 512]) for i in range(2)]

    tracks = [Track(n, d) for n, d in [
        ("s_par", True), ("s_u", True), ("s_wst", True),
        ("s_wgu", True), ("s_wdf", True), ("s_wdb", True), ("s_hrd", True),
        ("s_sxr", True), ("s_hw", True), ("s_sw", True), ("s_sd", True),
        ("s_sr", True), ("s_wtw", True), ("s_ow", True), ("s_lnw", True),
        ("s_lbw", True), ("s_lnr", True), ("s_dbg", True),
        ("c_v", False), ("c_a", False), ("c_p", False), ("c_g", False),
    ]]

    GEND = f"gu_mm_{ICN * TB - 1}"
    NW_TOT = NW_GU + NW_D
    NW_PER = (NW_TOT + LB - 1) // LB

    p0seq = []
    for lb in range(LB):
        for g in range(NGR):
            p0seq.append(("xn", lb, g))
        if lb >= CB:
            for g in range(NGR):
                p0seq.append(("xq", lb - CB, g))
    p0par = [it[2] % 2 for it in p0seq]
    p0ev = {("xn", lb, g): f"xnT_ev_{lb}_{g}" for lb in range(LB)
            for g in range(NGR)}
    p0ev.update({("xq", tb, g): f"xqT_ev_{tb}_{g}" for tb in range(TB)
                 for g in range(NGR)})

    def p0guard(i):
        j = i - 1
        while j >= 0 and p0par[j] != p0par[i]:
            j -= 1
        return p0ev[p0seq[j]] if j >= 0 else None

    p0q = []
    for lb in range(LB):
        for g in range(NGR):
            p0q.append((lb, g))


    def body_sync(E, Q):
        Q.engine("sync")
        if POISON:
            Q.wait(E, "s_lbw", "poison_done")
        for dst, src, nm in [
            (PV("lunc"), lunc_d.ap(), None), (PV("lim"), lim_d.ap(), None),
            (PV("logd"), logd_d.ap(), None),
            (bretile[:], bre_d.ap(), None), (bimtile[:], bim_d.ap(), None),
            (dbcast[0:1, :], dvec_d.ap(), "par_core"),
        ]:
            Q.inc(E.dma_start(out=dst, in_=src), "s_par", nm)
        for dc in range(DC):
            if dc > 0:
                Q.wait(E, "c_v", f"ctile_used_im_{dc - 1}")
            Q.inc(E.dma_start(out=ctile[:],
                              in_=cre_d.ap()[dc * 128:(dc + 1) * 128, :]),
                  "s_par", f"cre_{dc}")
            Q.wait(E, "c_v", f"ctile_used_re_{dc}")
            Q.inc(E.dma_start(out=ctile[:],
                              in_=cim_d.ap()[dc * 128:(dc + 1) * 128, :]),
                  "s_par", f"cim_{dc}")

        def wst_dma(k):
            if k < NW_GU:
                wt = wgT if k < NW_GU // 2 else wuT
                kk = k % (NW_GU // 2)
                rb, q = kk // (I // WCW), kk % (I // WCW)
                src = wt.ap()[rb * 128:(rb + 1) * 128,
                              q * WCW:(q + 1) * WCW]
            else:
                kk = k - NW_GU
                rb, q = kk // (D // WSD), kk % (D // WSD)
                src = wdT.ap()[rb * 128:(rb + 1) * 128,
                               q * WSD:(q + 1) * WSD]
            if k >= 3:
                p = k - 3
                nm = f"wstp_{p}" if p < NW_GU else f"wstdp_{p - NW_GU}"
                Q.wait(E, "c_a", nm)
            Q.inc(E.dma_start(out=wst[k % 3][:], in_=src),
                  "s_wst", f"wst_{k}")

        for lb in range(LB):
            if lb >= 2:
                Q.wait(E, "c_a", f"xn_done_{lb - 2}")
            src = (u_ctx.ap()[lb * 128:(lb + 1) * 128, :] if lb < CB
                   else u_own.ap()[(lb - CB) * 128:(lb - CB) * 128 + 128, :])
            Q.inc(E.dma_start(out=u_stage[lb % 2][:], in_=src), "s_u", f"u_{lb}")
            for k in range(NW_PER * lb, min(NW_PER * (lb + 1), NW_TOT)):
                wst_dma(k)
        for k in range(NW_PER * LB, NW_TOT):
            wst_dma(k)
        Q.wait(E, "c_v", f"s_asm_{TB - 1}")
        Q.wait(E, "c_p", f"y_mm_{TB - 1}")
        Q.wait(E, "s_sw", f"sw_{TB - 1}")
        if DEBUG:
            dssm = dbg_ssm.ap()
            for i, (tt, w) in enumerate([(rotc, 1), (rots, 1), (atile, 1),
                                         (bu, 2), (vt_t, 2)]):
                pass
            Q.inc(E.dma_start(out=dssm[:, 0:L], in_=rotc[:]), "s_par")
            Q.inc(E.dma_start(out=dssm[:, L:2 * L], in_=rots[:]), "s_par")
            Q.inc(E.dma_start(out=dssm[:, 2 * L:3 * L], in_=atile[:]), "s_par")
            Q.inc(E.dma_start(out=dssm[:, 3 * L:5 * L], in_=bu[:]), "s_par")
            Q.inc(E.dma_start(out=dssm[:, 5 * L:7 * L], in_=vt_t[:]),
                  "s_par")
            Q.inc(E.dma_start(out=dbg_par.ap(), in_=par[:],
                              allow_slow_non_contiguous=True),
                  "s_par", "dbgssm")
            Q.wait(E, "s_par", "dbgssm")
        for ic in range(ICN):
            if ic >= 2:
                Q.wait(E, "c_p", f"lnwq_mm_{ic - 2}")
            Q.inc(E.dma_start(out=lnw_sl[ic % 2][:],
                              in_=lnw_d.ap()[0:1, ic * 512:(ic + 1) * 512]),
                  "s_lnw", f"lnw_{ic}")
            for mi in range(2):
                wt = wgT if mi == 0 else wuT
                for q in range(QN):
                    gqi = (ic * 2 + mi) * QN + q
                    if gqi >= 2:
                        Q.wait(E, "c_a", f"sgn_gu_{gqi - 2}")
                    src3 = wt.ap().rearrange("(dc p) i -> p dc i", p=128)[
                        :, q * QDC:(q + 1) * QDC, ic * 512:(ic + 1) * 512]
                    dst3 = wguf[gqi % 2][:].rearrange("p (dc f) -> p dc f",
                                                      f=512)
                    Q.inc(E.dma_start(out=dst3, in_=src3), "s_wgu",
                          f"wgu_{gqi}")
        for sb in range(NSB):
            for ch in range(TBS * HCH):
                tbp, c = ch // HCH, ch % HCH
                tb = sb * TBS + tbp
                gi = sb * TBS * HCH + ch
                if gi == 0:
                    Q.wait(E, "c_p", GEND)
                    Q.wait(E, "s_lbw", f"lbw_{ICN - 1}")
                    if DEBUG:
                        Q.wait(E, "s_hw", f"hw_{ICN - 1}_{NHSG - 1}")
                        Q.wait(E, "s_sw", f"sw_{TB - 1}")
                        Q.inc(E.dma_start(out=dbg_xint.ap(), in_=x_intT[:]),
                              "s_par")
                        Q.inc(E.dma_start(out=dbg_s.ap(), in_=s_scr.ap()),
                              "s_par")
                        Q.inc(E.dma_start(out=dbg_h.ap(), in_=h_scr.ap()),
                              "s_par")
                        Q.inc(E.dma_start(out=dbg_lnw.ap(), in_=lnw_scr.ap()),
                              "s_par")
                        dsts = dbg_stats.ap()
                        for i, tt in enumerate([invsx, sumsq, rm2, cu, cu]):
                            Q.inc(E.dma_start(out=dsts[:, i * TB:(i + 1) * TB],
                                              in_=tt[:],
                                              allow_slow_non_contiguous=True),
                                  "s_par")
                        off = 5 * TB
                        for i, tt in enumerate([wsum_g, wsum_u, wsum_d,
                                                swg_bc, swu_bc, mg_bc, mu_bc]):
                            Q.inc(E.dma_start(out=dsts[:, off + i:off + i + 1],
                                              in_=tt[:],
                                              allow_slow_non_contiguous=True),
                                  "s_par")
                        Q.inc(E.dma_start(out=dsts[:, off + 8:off + 28],
                                          in_=stat1[:],
                                          allow_slow_non_contiguous=True),
                              "s_par", "dbg1")
                        Q.wait(E, "s_par", "dbg1")
                if gi >= 2:
                    Q.wait(E, "c_v", f"hy_{gi - 2}")
                Q.inc(E.dma_start(out=lnr_sl[gi % 2][:],
                                  in_=lnw_scr.ap()[:, c * HCW:(c + 1) * HCW]),
                      "s_lnr", f"lnr_{gi}")
                Q.wait(E, "s_hw",
                       f"hw_{(c + 1) * HCW // 512 - 1}_{tb // HSG}")
                Q.inc(E.dma_start(out=hrd[gi % 2][:],
                                  in_=h_scr.ap()[tb * 128:(tb + 1) * 128,
                                                 c * HCW:(c + 1) * HCW]),
                      "s_hrd", f"hrd_{gi}")
            for hh in range(HH):
                if DEBUG and sb == 0 and hh == 0:
                    Q.wait(E, "s_wtw", f"wtw_{NWDR - 1}")
                    Q.wait(E, "c_v", f"dscale_{TB - 1}")
                    Q.inc(E.dma_start(out=dbg_wdt.ap(), in_=wd_tern.ap()),
                          "s_par")
                    dsts = dbg_stats.ap()
                    off = 5 * TB + 30
                    Q.inc(E.dma_start(out=dsts[:, off:off + TB], in_=rsq[:],
                                      allow_slow_non_contiguous=True),
                          "s_par")
                    Q.inc(E.dma_start(out=dsts[:, off + TB:off + TB + 1],
                                      in_=md_bc[:],
                                      allow_slow_non_contiguous=True),
                          "s_par")
                    Q.inc(E.dma_start(out=dsts[:, off + TB + 1:off + TB + 2],
                                      in_=swd_bc[:],
                                      allow_slow_non_contiguous=True),
                          "s_par", "dbg2a")
                    Q.wait(E, "s_par", "dbg2a")
                if (DEBUG or DBGMD) and sb == 0 and hh == 0:
                    Q.wait(E, "s_wtw", f"wtw_{NWDR - 1}")
                    Q.inc(E.dma_start(out=dbg_wsd.ap()[:, 0:NW_D],
                                      in_=wsums_d[:, 0:NW_D]), "s_par")
                    Q.inc(E.dma_start(out=dbg_wsd.ap()[:, 256:257],
                                      in_=wsum_d[:],
                                      allow_slow_non_contiguous=True),
                          "s_par")
                    Q.inc(E.dma_start(out=dbg_wsd.ap()[0:1, 257:258],
                                      in_=sc11[0:1, 2:3],
                                      allow_slow_non_contiguous=True),
                          "s_par")
                    Q.inc(E.dma_start(out=dbg_wsd.ap()[:, 258:259],
                                      in_=md_bc[:],
                                      allow_slow_non_contiguous=True),
                          "s_par")
                    Q.inc(E.dma_start(out=dbg_wsd.ap()[:, 259:260],
                                      in_=swd_bc[:],
                                      allow_slow_non_contiguous=True),
                          "s_par", "dbg2")
                    Q.wait(E, "s_par", "dbg2")
                for icI in range(ICI):
                    wi = (sb * HH + hh) * ICI + icI
                    if wi >= NWDB:
                        Q.wait(E, "c_p", f"dmm_{wi - NWDB}")
                    Q.wait(E, "s_wtw", f"wtw_{icI * HH + hh}")
                    Q.inc(E.dma_start(out=wdb[wi % NWDB][:],
                                      in_=wd_tern.ap()[icI * 128:(icI + 1) * 128,
                                                       hh * HW:(hh + 1) * HW]),
                          "s_wdb", f"wdb_{wi}")

    def body_gpsimd(E, Q):
        Q.engine("gpsimd")
        if POISON:
            PZVAL = 1.0e30
            step = 8192
            last = None
            for c0 in range(0, PZC, step):
                last = E.memset(pzf[:, c0:min(c0 + step, PZC)], PZVAL)
            Q.fence(E, last, "c_g", "pz_sb")
            for nb in range(T // 128):
                Q.inc(E.dma_start(out=h_scr.ap()[nb * 128:(nb + 1) * 128, :],
                                  in_=pzf[:, 0:I]), "s_lbw")
                Q.inc(E.dma_start(out=s_scr.ap()[nb * 128:(nb + 1) * 128, :],
                                  in_=pzf[:, 0:D]), "s_lbw")
            Q.inc(E.dma_start(out=lnw_scr.ap(), in_=pzf[:, 0:I]), "s_lbw")
            for nb in range(I // 128):
                nm = "poison_done" if nb == I // 128 - 1 else None
                Q.inc(E.dma_start(out=wd_tern.ap()[nb * 128:(nb + 1) * 128, :],
                                  in_=pzb[:, 0:D]), "s_lbw", nm)
        E.memset(identw[:], 0.0)
        Q.fence(E, E.affine_select(out=identw[:], in_=identw[:],
                                   compare_op=ALU.not_equal, fill=1.0, base=0,
                                   pattern=[[-1, 128]], channel_multiplier=1),
                "c_g")
        E.memset(ones1[:], 1.0)
        E.memset(ones128[:], 1.0)
        E.memset(sumsq[:], 0.0)
        E.memset(rm2[:], 0.0)
        E.memset(chalf_n[:], -0.5)
        E.memset(chalf_p[:], 0.5)
        E.memset(rotc[:, 0:1], 1.0)
        Q.inc(E.memset(rots[:, 0:1], 0.0), "c_g", "init")
        Q.wait(E, "c_v", "bbar_bf")
        Q.wait(E, "c_v", f"qch_{TB - 1}_{NQ - 1}")
        Q.inc(E.memset(atile[:], 1.0), "c_g", "atile_set")
        for ic in range(ICN):
            Q.wait(E, "c_v", f"lnwq_cp_{ic}")
            Q.inc(E.dma_start(out=lnw_scr.ap()[:, ic * 512:(ic + 1) * 512],
                              in_=lnwq_bc[ic % 2][:]), "s_lbw", f"lbw_{ic}")
            for gr in range(NHSG):
                Q.wait(E, "c_v", f"h_{ic}_{gr * HSG + HSG - 1}")
                hbi = ic * NHSG + gr
                src3 = h_stage[hbi % 2][:].rearrange("p (tb f) -> p tb f",
                                                     f=512)
                dst4 = h_scr.ap().rearrange("(tb p) (ic f) -> ic p tb f",
                                            p=128, f=512)[ic][
                    :, gr * HSG:(gr + 1) * HSG, :]
                Q.inc(E.dma_start(out=dst4, in_=src3), "s_hw", f"hw_{ic}_{gr}")
        for sb in range(NSB):
            nch = TBS * HCH
            for ch in range(nch):
                gi = sb * nch + ch
                Q.wait(E, "c_v", f"hy_{gi}")
                if gi >= 2:
                    Q.wait(E, "c_v", f"hq_done_{gi - 2}")
                E.tensor_copy(hq_i32[gi % 2][:], hy[gi % 2][:])
                Q.inc(E.tensor_copy(hq_tf[gi % 2][:], hq_i32[gi % 2][:]),
                      "c_g", f"hcvt_{gi}")

    def body_scalar(E, Q):
        Q.engine("scalar")
        if POISON:
            Q.wait(E, "s_lbw", "poison_done")
        Q.wait(E, "s_par", "par_core")
        FA = lambda inst, nm=None: Q.fence(E, inst, "c_a", nm)
        FA(E.activation(out=PV("e1"), in_=PV("lunc"), func=AF.Exp, bias=0.0,
                        scale=1.0))
        FA(E.activation(out=PV("sp"), in_=PV("e1"), func=AF.Ln, bias=1.0,
                        scale=1.0))
        FA(E.activation(out=PV("dlt"), in_=PV("logd"), func=AF.Exp,
                        bias=0.0, scale=1.0), "par_a1")
        Q.wait(E, "c_v", "par_v1")
        FA(E.activation(out=PV("r"), in_=PV("lred"), func=AF.Exp, bias=0.0,
                        scale=1.0), "par_a2")
        Q.wait(E, "c_v", "par_v2")
        FA(E.activation(out=PV("sth"), in_=PV("thw"), func=AF.Sin, bias=0.0,
                        scale=1.0))
        FA(E.activation(out=PV("cth"), in_=PV("thc"), func=AF.Sin, bias=0.0,
                        scale=1.0), "par_a3")
        qflat = 0
        for lb in range(LB):
            own = lb >= CB
            tb = lb - CB
            Q.wait(E, "s_u", f"u_{lb}")
            if lb >= 2:
                Q.wait(E, "c_v", f"lnv_{lb - 2}")
            Q.inc(E.activation(out=dummy[:, 0:D], in_=u_stage[lb % 2][:],
                               func=AF.Square, bias=0.0, scale=1.0,
                               accum_out=stat1[:, (lb % 2):(lb % 2) + 1]),
                  "c_a", f"usq_{lb}")
            Q.wait(E, "c_v", f"lnv_{lb}")
            Q.inc(E.activation(out=stat1[:, 8 + lb % 2:9 + lb % 2],
                               in_=stat1[:, 6 + lb % 2:7 + lb % 2],
                               func=AF.Sqrt, bias=0.0, scale=1.0),
                  "c_a", f"sqv_{lb}")
            Q.wait(E, "c_v", f"lnstat_{lb}")
            Q.inc(E.activation(out=xn1[:], in_=u_stage[lb % 2][:],
                               func=AF.Identity,
                               bias=ln_nb[lb % 2][:],
                               scale=ln_sc[lb % 2][:]),
                  "c_a", f"xn_done_{lb}")
            for g in range(NGR):
                if qflat >= 2:
                    plb, pg = p0q[qflat - 2]
                    Q.wait(E, "c_p", f"xnT_{plb}_{pg}")
                Q.inc(E.activation(out=xbf_q[g % 2][:],
                                   in_=xn1[:, g * 512:(g + 1) * 512],
                                   func=AF.Copy, bias=0.0, scale=1.0),
                      "c_a", f"xbfq_{lb}_{g}")
                qflat += 1
            if own:
                for hf in range(NQ):
                    Q.wait(E, "c_v", f"sx_{tb}")
                    if hf >= 1:
                        Q.wait(E, "c_v", f"qch_{tb}_{hf - 1}")
                    Q.inc(E.activation(out=qy[:],
                                       in_=xn1[:, hf * 512:(hf + 1) * 512],
                                       func=AF.Copy, bias=0.0,
                                       scale=q_sc[:]),
                          "c_a", f"qy_{tb}_{hf}")
                Q.wait(E, "c_v", f"sd_{tb}")
                Q.inc(E.dma_start(out=s_scr.ap()[tb * 128:(tb + 1) * 128, :],
                                  in_=sd_scr[:]), "s_sd", f"sd_w_{tb}")
                if DEBUG:
                    Q.inc(E.dma_start(out=dbg_xn.ap()[tb * 128:(tb + 1) * 128, :],
                                      in_=xn1[:]), "s_dbg", f"dxn_{tb}")
                    Q.wait(E, "s_dbg", f"dxn_{tb}")
                    Q.wait(E, "c_v", f"qch_{tb}_{NQ - 1}")
                    Q.inc(E.dma_start(out=dbg_qneg.ap()[tb * 128:(tb + 1) * 128, :],
                                      in_=qneg[:]), "s_dbg", f"dqn_{tb}")
                    Q.wait(E, "s_dbg", f"dqn_{tb}")
            for k in range(NW_PER * lb, min(NW_PER * (lb + 1), NW_TOT)):
                Q.wait(E, "s_wst", f"wst_{k}")
                acc = (wsums_gu[:, k:k + 1] if k < NW_GU
                       else wsums_d[:, k - NW_GU:k - NW_GU + 1])
                nm = f"wstp_{k}" if k < NW_GU else f"wstdp_{k - NW_GU}"
                Q.inc(E.activation(out=dummy[:, 0:WCW], in_=wst[k % 3][:],
                                   func=AF.Abs, bias=0.0, scale=1.0,
                                   accum_out=acc),
                      "c_a", nm)
        for k in range(NW_PER * LB, NW_TOT):
            Q.wait(E, "s_wst", f"wst_{k}")
            acc = (wsums_gu[:, k:k + 1] if k < NW_GU
                   else wsums_d[:, k - NW_GU:k - NW_GU + 1])
            nm = f"wstp_{k}" if k < NW_GU else f"wstdp_{k - NW_GU}"
            Q.inc(E.activation(out=dummy[:, 0:WCW], in_=wst[k % 3][:],
                               func=AF.Abs, bias=0.0, scale=1.0,
                               accum_out=acc), "c_a", nm)
        for tb in range(TB):
            Q.wait(E, "s_sd", f"sd_w_{tb}")
            if tb == 0:
                Q.wait(E, "c_p", f"btp_{DC - 1}")
            if tb >= 1:
                Q.wait(E, "s_sw", f"sw_{tb - 1}")
            Q.inc(E.dma_start(out=s_stage[:],
                              in_=s_scr.ap()[tb * 128:(tb + 1) * 128, :]),
                  "s_sr", f"sr_{tb}")
            Q.wait(E, "c_v", f"s_asm_{tb}")
            Q.inc(E.dma_start(out=s_scr.ap()[tb * 128:(tb + 1) * 128, :],
                              in_=s_stage[:]), "s_sw", f"sw_{tb}")
        for ic in range(ICN):
            for mi in range(2):
                sw = swg_bc if mi == 0 else swu_bc
                for q in range(QN):
                    gqi = (ic * 2 + mi) * QN + q
                    Q.wait(E, "s_wgu", f"wgu_{gqi}")
                    if gqi == 0:
                        Q.wait(E, "c_v", "sw_ready")
                    if gqi >= 2:
                        Q.wait(E, "c_v", f"tern_gu_{gqi - 2}")
                    E.activation(out=tern1[gqi % 2][:], in_=wguf[gqi % 2][:],
                                 func=AF.Sign, bias=chalf_n[:], scale=sw[:])
                    Q.inc(E.activation(out=tern2[gqi % 2][:],
                                       in_=wguf[gqi % 2][:], func=AF.Sign,
                                       bias=chalf_p[:], scale=sw[:]),
                          "c_a", f"sgn_gu_{gqi}")
            for tb in range(TB):
                gi = ic * TB + tb
                Q.wait(E, "c_p", f"gu_mm_{gi}")
                Q.inc(E.activation(out=g_stage[gi % 2][:],
                                   in_=pb[0 + gi % 2][:], func=AF.Silu,
                                   bias=0.0, scale=cg_l[tb][:]),
                      "c_a", f"gsilu_{gi}")
                hbi = ic * NHSG + tb // HSG
                Q.wait(E, "c_v", f"h_{ic}_{tb}")
                if gi >= 2:
                    Q.wait(E, "c_v", f"hstat_{gi - 2}")
                Q.inc(E.activation(
                    out=hsq[gi % 2][:],
                    in_=h_stage[hbi % 2][:, (tb % HSG) * 512:(tb % HSG) * 512 + 512],
                    func=AF.Square, bias=0.0, scale=1.0,
                    accum_out=stat1[:, 17 + gi % 2:18 + gi % 2]),
                    "c_a", f"hsq_{gi}")
        for tb in range(TB):
            Q.wait(E, "c_v", f"dstat1_{tb}")
            E.activation(out=stat1[:, 8:9], in_=stat1[:, 6:7], func=AF.Sqrt,
                         bias=0.0, scale=1.0)
            Q.inc(E.activation(out=stat1[:, 9:10], in_=rm2[:, tb:tb + 1],
                               func=AF.Sqrt, bias=0.0, scale=1.0),
                  "c_a", f"dsq_{tb}")
            Q.wait(E, "c_v", f"dscale_{tb}")
        for kk in range(NWDR):
            rb, hhp = kk // HH, kk % HH
            if kk == 0:
                Q.wait(E, "c_p", GEND)
                Q.wait(E, "c_v", "swd_ready")
            if kk >= 2:
                Q.wait(E, "c_v", f"tern_wd_{kk - 2}")
            Q.inc(E.dma_start(out=wdf[kk % 2][:],
                              in_=wdT.ap()[rb * 128:(rb + 1) * 128,
                                           hhp * WDW:(hhp + 1) * WDW]),
                  "s_wdf", f"wdf_{kk}")
            Q.wait(E, "s_wdf", f"wdf_{kk}")
            if kk >= 1:
                Q.wait(E, "c_v", f"tern_wd_{kk - 1}")
            E.activation(out=wdsg1[:], in_=wdf[kk % 2][:], func=AF.Sign,
                         bias=chalf_n[:], scale=swd_bc[:])
            Q.inc(E.activation(out=wdsg2[:], in_=wdf[kk % 2][:], func=AF.Sign,
                               bias=chalf_p[:], scale=swd_bc[:]),
                  "c_a", f"sgn_wd_{kk}")
            Q.wait(E, "c_v", f"tern_wd_{kk}")
            Q.inc(E.dma_start(out=wd_tern.ap()[rb * 128:(rb + 1) * 128,
                                               hhp * WDW:(hhp + 1) * WDW],
                              in_=wdt_bf[kk % 2][:]), "s_wtw", f"wtw_{kk}")
        ehf = 0
        for sb in range(NSB):
            for hh in range(HH):
                for tbp in range(TBS):
                    tb = sb * TBS + tbp
                    for half in range(NHF):
                        if ehf >= 2:
                            Q.wait(E, "c_v", f"oeps_{ehf - 2}")
                        Q.inc(E.dma_start(
                            out=s_sl[ehf % 2][:],
                            in_=s_scr.ap()[tb * 128:(tb + 1) * 128,
                                           hh * HW + half * 512:
                                           hh * HW + (half + 1) * 512]),
                            "s_sxr")
                        Q.inc(E.dma_start(
                            out=x_sl[ehf % 2][:],
                            in_=u_own.ap()[tb * 128:(tb + 1) * 128,
                                           hh * HW + half * 512:
                                           hh * HW + (half + 1) * 512]),
                            "s_sxr", f"sxr_{ehf}")
                        Q.wait(E, "c_p", f"dmm_{(sb * HH + hh) * ICI + ICI - 1}")
                        Q.inc(E.activation(out=go_stage[ehf % 2][:],
                                           in_=pb[NHF * tbp + half][:],
                                           func=AF.Silu, bias=0.0,
                                           scale=cdq_l[tb][:]),
                              "c_a", f"dsilu_{ehf}")
                        Q.wait(E, "c_v", f"oeps_{ehf}")
                        Q.inc(E.dma_start(
                            out=out_d.ap()[tb * 128:(tb + 1) * 128,
                                           hh * HW + half * 512:
                                           hh * HW + (half + 1) * 512],
                            in_=o_stage[ehf % 2][:]), "s_ow", f"ow_{ehf}")
                        ehf += 1
        Q.wait_total(E, "s_ow")

    def body_vector(E, Q):
        Q.engine("vector")
        if POISON:
            Q.wait(E, "s_lbw", "poison_done")
        Q.wait(E, "c_g", "init")
        F = lambda inst, nm=None: Q.fence(E, inst, "c_v", nm)
        F(E.tensor_copy(ident_bf[:], identw[:]))
        F(E.tensor_copy(identNH_bf[:], identw[0:NH, 0:NH]), "ident_bf")
        for q in range(NQ):
            Q.wait(E, "c_p", f"dbc_mm_{q}")
            Q.inc(E.tensor_copy(dbcast[:, q * 512:(q + 1) * 512], pb[4][:]),
                  "c_v", f"dbc_ev_{q}")
        Q.wait(E, "c_a", "par_a1")
        F(E.tensor_scalar(out=PV("lre"), in0=PV("sp"), scalar1=0.0101,
                          scalar2=-1.0, op0=ALU.add, op1=ALU.mult))
        F(E.tensor_tensor(out=PV("lred"), in0=PV("lre"), in1=PV("dlt"),
                          op=ALU.mult))
        F(E.tensor_tensor(out=PV("th"), in0=PV("lim"), in1=PV("dlt"),
                          op=ALU.mult), "par_v1")
        F(E.tensor_scalar(out=PV("t1"), in0=PV("th"), scalar1=PI, scalar2=None,
                          op0=ALU.is_gt))
        F(E.tensor_scalar(out=PV("t2"), in0=PV("th"), scalar1=-PI,
                          scalar2=None, op0=ALU.is_lt))
        F(E.tensor_tensor(out=PV("t3"), in0=PV("t1"), in1=PV("t2"),
                          op=ALU.subtract))
        F(E.scalar_tensor_tensor(out=PV("thw"), in0=PV("t3"),
                                 scalar=-TWO_PI_HI, in1=PV("th"), op0=ALU.mult,
                                 op1=ALU.add))
        F(E.scalar_tensor_tensor(out=PV("thw"), in0=PV("t3"),
                                 scalar=-TWO_PI_LO, in1=PV("thw"),
                                 op0=ALU.mult, op1=ALU.add))
        F(E.tensor_scalar(out=PV("t4"), in0=PV("thw"), scalar1=-1.0,
                          scalar2=HALF_PI_HI, op0=ALU.mult, op1=ALU.add))
        F(E.tensor_scalar(out=PV("t1"), in0=PV("t4"), scalar1=PI, scalar2=None,
                          op0=ALU.is_gt))
        F(E.scalar_tensor_tensor(out=PV("thc"), in0=PV("t1"),
                                 scalar=-TWO_PI_HI, in1=PV("t4"), op0=ALU.mult,
                                 op1=ALU.add))
        F(E.tensor_scalar(out=PV("thc"), in0=PV("thc"), scalar1=HALF_PI_LO,
                          scalar2=None, op0=ALU.add), "par_v2")
        Q.wait(E, "c_a", "par_a3")
        F(E.tensor_tensor(out=PV("t1"), in0=PV("r"), in1=PV("cth"),
                          op=ALU.mult))
        F(E.tensor_tensor(out=PV("t2"), in0=PV("r"), in1=PV("sth"),
                          op=ALU.mult))
        F(E.tensor_scalar(out=PV("nre"), in0=PV("t1"), scalar1=-1.0,
                          scalar2=None, op0=ALU.add))
        F(E.tensor_tensor(out=PV("t3"), in0=PV("lre"), in1=PV("lre"),
                          op=ALU.mult))
        F(E.tensor_tensor(out=PV("t4"), in0=PV("lim"), in1=PV("lim"),
                          op=ALU.mult))
        F(E.tensor_tensor(out=PV("den"), in0=PV("t3"), in1=PV("t4"),
                          op=ALU.add))
        F(E.reciprocal(out=PV("dinv"), in_=PV("den")))
        F(E.tensor_tensor(out=PV("t3"), in0=PV("nre"), in1=PV("lre"),
                          op=ALU.mult))
        F(E.tensor_tensor(out=PV("t4"), in0=PV("t2"), in1=PV("lim"),
                          op=ALU.mult))
        F(E.tensor_tensor(out=PV("wre"), in0=PV("t3"), in1=PV("t4"),
                          op=ALU.add))
        F(E.tensor_tensor(out=PV("wre"), in0=PV("wre"), in1=PV("dinv"),
                          op=ALU.mult))
        F(E.tensor_tensor(out=PV("t3"), in0=PV("t2"), in1=PV("lre"),
                          op=ALU.mult))
        F(E.tensor_tensor(out=PV("t4"), in0=PV("nre"), in1=PV("lim"),
                          op=ALU.mult))
        F(E.tensor_tensor(out=PV("wim"), in0=PV("t3"), in1=PV("t4"),
                          op=ALU.subtract))
        F(E.tensor_tensor(out=PV("wim"), in0=PV("wim"), in1=PV("dinv"),
                          op=ALU.mult))
        F(E.tensor_scalar(out=bb1[:, 0:D], in0=bretile[:], scalar1=PV("wre"),
                          scalar2=None, op0=ALU.mult))
        F(E.tensor_scalar(out=bb2[:, 0:D], in0=bimtile[:], scalar1=PV("wim"),
                          scalar2=None, op0=ALU.mult))
        F(E.tensor_tensor(out=bb3[0:NH, :], in0=bb1[:, 0:D], in1=bb2[:, 0:D],
                          op=ALU.subtract))
        F(E.tensor_copy(bre_bf[:], bb3[0:NH, :]))
        F(E.tensor_scalar(out=bb1[:, 0:D], in0=bimtile[:], scalar1=PV("wre"),
                          scalar2=None, op0=ALU.mult))
        F(E.tensor_scalar(out=bb2[:, 0:D], in0=bretile[:], scalar1=PV("wim"),
                          scalar2=None, op0=ALU.mult))
        F(E.tensor_tensor(out=bb3[0:NH, :], in0=bb1[:, 0:D], in1=bb2[:, 0:D],
                          op=ALU.add))
        Q.inc(E.tensor_copy(bim_bf[:], bb3[0:NH, :]), "c_v", "bbar_bf")
        if L > 1:
            F(E.tensor_copy(rotc[:, 1:2], PV("cth")))
            F(E.tensor_copy(rots[:, 1:2], PV("sth")))
        n = 1
        while n < L - 1:
            k = min(n, L - 1 - n)
            cm, sm = rotc[:, n:n + 1], rots[:, n:n + 1]
            F(E.tensor_scalar(out=vv1[:, 0:k], in0=rots[:, 1:k + 1],
                              scalar1=sm, scalar2=None, op0=ALU.mult))
            F(E.scalar_tensor_tensor(out=vv2[:, 0:k], in0=rotc[:, 1:k + 1],
                                     scalar=cm, in1=vv1[:, 0:k], op0=ALU.mult,
                                     op1=ALU.subtract))
            F(E.tensor_scalar(out=vv1[:, 0:k], in0=rots[:, 1:k + 1],
                              scalar1=cm, scalar2=None, op0=ALU.mult))
            F(E.scalar_tensor_tensor(out=rots[:, n + 1:n + k + 1],
                                     in0=rotc[:, 1:k + 1], scalar=sm,
                                     in1=vv1[:, 0:k], op0=ALU.mult,
                                     op1=ALU.add))
            F(E.tensor_copy(rotc[:, n + 1:n + k + 1], vv2[:, 0:k]))
            n += k
        for dc in range(DC):
            Q.wait(E, "s_par", f"cre_{dc}")
            Q.inc(E.tensor_scalar(out=ctile_bf[:], in0=ctile[:], scalar1=2.0,
                                  scalar2=None, op0=ALU.mult),
                  "c_v", f"ctile_used_re_{dc}")
            Q.wait(E, "c_p", f"ctp_re_{dc}")
            Q.inc(E.tensor_copy(ctre_bf[:, dc * 128:(dc + 1) * 128],
                                vts[0][0:NH, 0:128]), "c_v", f"ctre_ev_{dc}")
            Q.wait(E, "s_par", f"cim_{dc}")
            Q.inc(E.tensor_scalar(out=ctile_bf[:], in0=ctile[:], scalar1=-2.0,
                                  scalar2=None, op0=ALU.mult),
                  "c_v", f"ctile_used_im_{dc}")
            Q.wait(E, "c_p", f"ctp_im_{dc}")
            Q.inc(E.tensor_copy(ctim_bf[:, dc * 128:(dc + 1) * 128],
                                vts[1][0:NH, 0:128]), "c_v", f"ctim_ev_{dc}")
        for dc in range(DC):
            Q.wait(E, "c_p", f"btp_{dc}")
            E.tensor_copy(bretT[:, dc * NH:(dc + 1) * NH], vts[0][0:128, 0:NH])
            Q.inc(E.tensor_copy(bimtT[:, dc * NH:(dc + 1) * NH],
                                vts[1][0:128, 0:NH]), "c_v", f"btp_ev_{dc}")
        for lb in range(LB):
            own = lb >= CB
            tb = lb - CB
            Q.wait(E, "s_u", f"u_{lb}")
            F(E.reduce_sum(out=stat1[:, 2 + lb % 2:3 + lb % 2],
                           in_=u_stage[lb % 2][:], axis=AX.X))
            Q.wait(E, "c_a", f"usq_{lb}")
            F(E.tensor_scalar(out=stat1[:, 4 + lb % 2:5 + lb % 2],
                              in0=stat1[:, 2 + lb % 2:3 + lb % 2],
                              scalar1=INV_D, scalar2=None, op0=ALU.mult))
            F(E.tensor_scalar(out=stat1[:, 6 + lb % 2:7 + lb % 2],
                              in0=stat1[:, lb % 2:lb % 2 + 1], scalar1=INV_D,
                              scalar2=None, op0=ALU.mult))
            F(E.tensor_tensor(out=stat1[:, 19:20],
                              in0=stat1[:, 4 + lb % 2:5 + lb % 2],
                              in1=stat1[:, 4 + lb % 2:5 + lb % 2],
                              op=ALU.mult))
            F(E.tensor_tensor(out=stat1[:, 6 + lb % 2:7 + lb % 2],
                              in0=stat1[:, 6 + lb % 2:7 + lb % 2],
                              in1=stat1[:, 19:20], op=ALU.subtract))
            F(E.tensor_scalar(out=stat1[:, 6 + lb % 2:7 + lb % 2],
                              in0=stat1[:, 6 + lb % 2:7 + lb % 2],
                              scalar1=1e-5, scalar2=None, op0=ALU.add),
              f"lnv_{lb}")
            Q.wait(E, "c_a", f"sqv_{lb}")
            F(E.reciprocal(out=ln_sc[lb % 2][:],
                           in_=stat1[:, 8 + lb % 2:9 + lb % 2]))
            F(E.scalar_tensor_tensor(out=ln_nb[lb % 2][:],
                                     in0=stat1[:, 4 + lb % 2:5 + lb % 2],
                                     scalar=-1.0, in1=ln_sc[lb % 2][:],
                                     op0=ALU.mult, op1=ALU.mult),
              f"lnstat_{lb}")
            if own:
                Q.wait(E, "c_a", f"xn_done_{lb}")
                F(E.reduce_max(out=stat1[:, 15:16], in_=xn1[:], axis=AX.X,
                               apply_absolute_value=True))
                F(E.tensor_scalar(out=stat1[:, 15:16], in0=stat1[:, 15:16],
                                  scalar1=1e-4, scalar2=None, op0=ALU.max))
                F(E.tensor_scalar(out=invsx[:, tb:tb + 1],
                                  in0=stat1[:, 15:16],
                                  scalar1=float(np.float32(1.0 / 127.0)),
                                  scalar2=None, op0=ALU.mult))
                F(E.reciprocal(out=stat1[:, 14:15], in_=stat1[:, 15:16]))
                F(E.tensor_scalar(out=q_sc[:], in0=stat1[:, 14:15],
                                  scalar1=127.0, scalar2=None, op0=ALU.mult),
                  f"sx_{tb}")
                if tb >= 1:
                    Q.wait(E, "s_sd", f"sd_w_{tb - 1}")
                Q.inc(E.tensor_tensor(out=sd_scr[:], in0=xn1[:],
                                      in1=dbcast[:], op=ALU.mult),
                      "c_v", f"sd_{tb}")
                for hf in range(NQ):
                    Q.wait(E, "c_a", f"qy_{tb}_{hf}")
                    if hf == 0 and tb > 0:
                        Q.wait(E, "c_p", f"xqT_{tb - 1}_{NGR - 1}")
                    E.tensor_copy(qi32[:], qy[:])
                    E.tensor_copy(qtf[:], qi32[:])
                    E.tensor_tensor(out=qd[:], in0=qy[:], in1=qtf[:],
                                    op=ALU.subtract)
                    E.scalar_tensor_tensor(out=qa[:], in0=qd[:], scalar=0.5,
                                           in1=qtf[:], op0=ALU.is_gt,
                                           op1=ALU.add)
                    Q.inc(E.scalar_tensor_tensor(
                        out=qneg[:, hf * 512:(hf + 1) * 512], in0=qd[:],
                        scalar=-0.5, in1=qa[:], op0=ALU.is_lt,
                        op1=ALU.subtract), "c_v", f"qch_{tb}_{hf}")
            for g in range(NGR):
                nd = min(4, DC - g * 4)
                Q.wait(E, "c_p", f"xnT_{lb}_{g}")
                dst = xnT_tile[lb % 2][:].rearrange("p (dc f) -> p dc f",
                                                    f=128)[:, g * 4:g * 4 + nd, :]
                Q.inc(E.tensor_copy(dst, vts[g % 2][:, 0:128 * nd].rearrange(
                    "p (dc f) -> p dc f", f=128)), "c_v", f"xnT_ev_{lb}_{g}")
            if own:
                for g in range(NGR):
                    nd = min(4, DC - g * 4)
                    Q.wait(E, "c_p", f"xqT_{tb}_{g}")
                    dst = x_intT[:].rearrange("p (dc f) -> p dc f", f=T)[
                        :, g * 4:g * 4 + nd, tb * 128:(tb + 1) * 128]
                    Q.inc(E.tensor_copy(dst, vts[g % 2][:, 0:128 * nd].rearrange(
                        "p (dc f) -> p dc f", f=128)), "c_v",
                        f"xqT_ev_{tb}_{g}")
            Q.wait(E, "c_p", f"bu_mm_{lb}")
            E.tensor_copy(bu[:, lb * 128:(lb + 1) * 128], pb[4][0:NH, 0:128])
            Q.inc(E.tensor_copy(bu[:, L + lb * 128:L + (lb + 1) * 128],
                                pb[5][0:NH, 0:128]), "c_v", f"bu_ev_{lb}")
        Q.wait(E, "c_a", f"wstp_{NW_GU - 1}")
        F(E.reduce_sum(out=wsum_g[:], in_=wsums_gu[:, 0:NW_GU // 2],
                       axis=AX.X))
        F(E.reduce_sum(out=wsum_u[:], in_=wsums_gu[:, NW_GU // 2:NW_GU],
                       axis=AX.X), f"wstacc_{NW_GU - 1}")
        Q.wait(E, "c_g", "atile_set")
        Q.inc(E.tensor_scalar(out=atile[:], in0=atile[:], scalar1=PV("r"),
                              scalar2=None, op0=ALU.mult), "c_v", "rot_done")
        E.tensor_tensor(out=vv1[:], in0=rotc[:], in1=bu[:, 0:L], op=ALU.mult)
        E.tensor_tensor(out=vv2[:], in0=rots[:], in1=bu[:, L:2 * L],
                        op=ALU.mult)
        E.tensor_tensor(out=vt_t[:, 0:L], in0=vv1[:], in1=vv2[:], op=ALU.add)
        E.tensor_tensor(out=vv1[:], in0=rotc[:], in1=bu[:, L:2 * L],
                        op=ALU.mult)
        E.tensor_tensor(out=vv2[:], in0=rots[:], in1=bu[:, 0:L], op=ALU.mult)
        E.tensor_tensor(out=vt_t[:, L:2 * L], in0=vv1[:], in1=vv2[:],
                        op=ALU.subtract)
        E.tensor_tensor_scan(out=zt[:, 0:L], data0=atile[:],
                             data1=vt_t[:, 0:L], initial=0.0, op0=ALU.mult,
                             op1=ALU.add)
        E.tensor_tensor_scan(out=zt[:, L:2 * L], data0=atile[:],
                             data1=vt_t[:, L:2 * L], initial=0.0,
                             op0=ALU.mult, op1=ALU.add)
        E.tensor_tensor(out=vv1[:], in0=rotc[:], in1=zt[:, 0:L], op=ALU.mult)
        E.tensor_tensor(out=vv2[:], in0=rots[:], in1=zt[:, L:2 * L],
                        op=ALU.mult)
        E.tensor_tensor(out=xs[:, 0:L], in0=vv1[:], in1=vv2[:],
                        op=ALU.subtract)
        E.tensor_tensor(out=vv1[:], in0=rotc[:], in1=zt[:, L:2 * L],
                        op=ALU.mult)
        E.tensor_tensor(out=vv2[:], in0=rots[:], in1=zt[:, 0:L], op=ALU.mult)
        E.tensor_tensor(out=xs[:, L:2 * L], in0=vv1[:], in1=vv2[:], op=ALU.add)
        E.tensor_copy(xs_bf[:, 0:T], xs[:, CTX:L])
        Q.inc(E.tensor_copy(xs_bf[:, T:2 * T], xs[:, L + CTX:2 * L]),
              "c_v", "xs_bf")
        for tb in range(TB):
            Q.wait(E, "c_p", f"y_mm_{tb}")
            Q.wait(E, "s_sr", f"sr_{tb}")
            for q in range(NQ):
                Q.inc(E.tensor_tensor(out=s_stage[:, q * 512:(q + 1) * 512],
                                      in0=pb[q % 4][:],
                                      in1=s_stage[:, q * 512:(q + 1) * 512],
                                      op=ALU.add), "c_v",
                      f"s_asm_{tb}" if q == NQ - 1 else None)
        Q.wait(E, "c_p", "wtot_mm")
        Q.inc(E.tensor_copy(sc11[0:1, 0:2], pb[5][0:1, 0:2]), "c_v",
              "wtot_ev")
        Q.wait(E, "c_p", "wtot_bc")
        for i, (mb, sb_) in enumerate([(mg_bc, swg_bc), (mu_bc, swu_bc)]):
            F(E.tensor_scalar(out=mb[:], in0=pb[4][:, i:i + 1], scalar1=INV_NW,
                              scalar2=1e-4, op0=ALU.mult, op1=ALU.max))
            F(E.reciprocal(out=sb_[:], in_=mb[:]))
        Q.inc(E.tensor_copy(dummy[0:1, 0:1], ones1[0:1, 0:1]), "c_v",
              "sw_ready")
        for tb in range(TB):
            F(E.scalar_tensor_tensor(out=cg_l[tb][:],
                                     in0=invsx[:, tb:tb + 1], scalar=-0.5,
                                     in1=mg_bc[:], op0=ALU.mult, op1=ALU.mult))
            F(E.scalar_tensor_tensor(out=cu[:, tb:tb + 1],
                                     in0=invsx[:, tb:tb + 1], scalar=-0.5,
                                     in1=mu_bc[:], op0=ALU.mult, op1=ALU.mult),
              f"cgu_{tb}" if tb == TB - 1 else None)
        for ic in range(ICN):
            for mi in range(2):
                for q in range(QN):
                    gqi = (ic * 2 + mi) * QN + q
                    Q.wait(E, "c_a", f"sgn_gu_{gqi}")
                    if ic >= 2 and mi == 0 and q == 0:
                        Q.wait(E, "c_p", f"gu_mm_{(ic - 1) * TB - 1}")
                    wdst = (wg_bf if mi == 0 else wu_bf)[ic % 2][
                        :, q * QDC * 512:(q + 1) * QDC * 512]
                    Q.inc(E.tensor_tensor(out=wdst, in0=tern1[gqi % 2][:],
                                          in1=tern2[gqi % 2][:], op=ALU.add),
                          "c_v", f"tern_gu_{gqi}")
            Q.wait(E, "c_p", f"lnwq_mm_{ic}")
            Q.inc(E.tensor_copy(lnwq_bc[ic % 2][:], pb[4][:, 0:512]), "c_v",
                  f"lnwq_cp_{ic}")
            Q.wait(E, "s_lbw", f"lbw_{ic}")
            Q.inc(E.tensor_tensor(out=lnwq_bc[ic % 2][:],
                                  in0=lnwq_bc[ic % 2][:],
                                  in1=lnwq_bc[ic % 2][:], op=ALU.mult),
                  "c_v", f"lnwq_ev_{ic}")
            for tb in range(TB):
                gi = ic * TB + tb
                hbi = ic * NHSG + tb // HSG
                Q.wait(E, "c_a", f"gsilu_{gi}")
                if hbi >= 2 and tb % HSG == 0:
                    Q.wait(E, "s_hw",
                           f"hw_{(hbi - 2) // NHSG}_{(hbi - 2) % NHSG}")
                Q.inc(E.scalar_tensor_tensor(
                    out=h_stage[hbi % 2][:, (tb % HSG) * 512:(tb % HSG) * 512 + 512],
                    in0=pb[2 + gi % 2][:], scalar=cu[:, tb:tb + 1],
                    in1=g_stage[gi % 2][:], op0=ALU.mult, op1=ALU.mult),
                    "c_v", f"h_{ic}_{tb}")
                Q.wait(E, "c_a", f"hsq_{gi}")
                F(E.tensor_tensor(out=sumsq[:, tb:tb + 1],
                                  in0=sumsq[:, tb:tb + 1],
                                  in1=stat1[:, 17 + gi % 2:18 + gi % 2],
                                  op=ALU.add))
                E.tensor_tensor(out=dummy[:, 1024:1536], in0=hsq[gi % 2][:],
                                in1=lnwq_bc[ic % 2][:], op=ALU.mult)
                F(E.reduce_max(out=stat1[:, 19:20], in_=dummy[:, 1024:1536], axis=AX.X))
                F(E.tensor_tensor(out=rm2[:, tb:tb + 1],
                                  in0=rm2[:, tb:tb + 1],
                                  in1=stat1[:, 19:20], op=ALU.max),
                  f"hstat_{gi}")
        Q.wait(E, "c_a", f"wstdp_{NW_D - 1}")
        F(E.reduce_sum(out=wsum_d[:], in_=wsums_d[:, 0:NW_D], axis=AX.X),
          f"wstdacc_{NW_D - 1}")
        Q.wait(E, "c_p", "wtotd_mm")
        Q.inc(E.tensor_copy(sc11[0:1, 2:3], pb[5][0:1, 2:3]), "c_v",
              "wtotd_ev")
        Q.wait(E, "c_p", "wtotd_bc")
        F(E.tensor_scalar(out=md_bc[:], in0=pb[4][:, 0:1], scalar1=INV_NW,
                          scalar2=1e-4, op0=ALU.mult, op1=ALU.max))
        F(E.reciprocal(out=swd_bc[:], in_=md_bc[:]))
        Q.inc(E.tensor_copy(dummy[0:1, 0:1], ones1[0:1, 0:1]), "c_v",
              "swd_ready")
        for tb in range(TB):
            Q.inc(E.tensor_scalar(out=stat1[:, 6:7], in0=sumsq[:, tb:tb + 1],
                                  scalar1=INV_I, scalar2=1e-6, op0=ALU.mult,
                                  op1=ALU.add), "c_v", f"dstat1_{tb}")
            Q.wait(E, "c_a", f"dsq_{tb}")
            F(E.reciprocal(out=stat1[:, 10:11], in_=stat1[:, 8:9]))
            F(E.tensor_tensor(out=stat1[:, 11:12], in0=stat1[:, 9:10],
                              in1=stat1[:, 10:11], op=ALU.mult))
            F(E.tensor_scalar(out=stat1[:, 11:12], in0=stat1[:, 11:12],
                              scalar1=1e-4, scalar2=None, op0=ALU.max))
            F(E.reciprocal(out=stat1[:, 12:13], in_=stat1[:, 11:12]))
            F(E.tensor_scalar(out=stat1[:, 12:13], in0=stat1[:, 12:13],
                              scalar1=127.0, scalar2=None, op0=ALU.mult))
            F(E.tensor_tensor(out=rsq[:, tb:tb + 1], in0=stat1[:, 10:11],
                              in1=stat1[:, 12:13], op=ALU.mult))
            F(E.scalar_tensor_tensor(out=cdq_l[tb][:], in0=stat1[:, 11:12],
                                     scalar=float(np.float32(-0.5 / 127.0)),
                                     in1=md_bc[:], op0=ALU.mult,
                                     op1=ALU.mult),
              f"dscale_{tb}")
        for kk in range(NWDR):
            Q.wait(E, "c_a", f"sgn_wd_{kk}")
            if kk >= 2:
                Q.wait(E, "s_wtw", f"wtw_{kk - 2}")
            Q.inc(E.tensor_tensor(out=wdt_bf[kk % 2][:], in0=wdsg1[:],
                                  in1=wdsg2[:], op=ALU.add),
                  "c_v", f"tern_wd_{kk}")
        ehf = 0
        for sb in range(NSB):
            nch = TBS * HCH
            for ch in range(nch):
                tbp, c = ch // HCH, ch % HCH
                tb = sb * TBS + tbp
                gi = sb * nch + ch
                Q.wait(E, "s_hrd", f"hrd_{gi}")
                Q.wait(E, "s_lnr", f"lnr_{gi}")
                Q.wait(E, "c_v", f"dscale_{tb}")
                Q.inc(E.scalar_tensor_tensor(
                    out=hy[gi % 2][:], in0=hrd[gi % 2][:],
                    scalar=rsq[:, tb:tb + 1], in1=lnr_sl[gi % 2][:],
                    op0=ALU.mult, op1=ALU.mult), "c_v", f"hy_{gi}")
                Q.wait(E, "c_g", f"hcvt_{gi}")
                if gi >= 2:
                    Q.wait(E, "c_p", f"qhT_{gi - 2}_{HGR - 1}")
                E.tensor_tensor(out=hqd[:], in0=hy[gi % 2][:],
                                in1=hq_tf[gi % 2][:], op=ALU.subtract)
                E.scalar_tensor_tensor(out=hqa[:], in0=hqd[:], scalar=0.5,
                                       in1=hq_tf[gi % 2][:], op0=ALU.is_gt,
                                       op1=ALU.add)
                Q.inc(E.scalar_tensor_tensor(out=qh_tok[gi % 2][:],
                                             in0=hqd[:], scalar=-0.5,
                                             in1=hqa[:], op0=ALU.is_lt,
                                             op1=ALU.subtract),
                      "c_v", f"hq_done_{gi}")
                for g in range(HGR):
                    Q.wait(E, "c_p", f"qhT_{gi}_{g}")
                    dst = qhT[:].rearrange("p (ii f) -> p ii f",
                                           f=TBS * 128)[
                        :, (c * HCW // 128) + g * 4:(c * HCW // 128) + g * 4 + 4,
                        tbp * 128:(tbp + 1) * 128]
                    Q.inc(E.tensor_copy(dst, vts[g % 2][:, 0:512].rearrange(
                        "p (ii f) -> p ii f", f=128)),
                        "c_v", f"qhT_ev_{gi}_{g}")
            for hh in range(HH):
                for tbp in range(TBS):
                    for half in range(NHF):
                        Q.wait(E, "c_a", f"dsilu_{ehf}")
                        Q.wait(E, "s_sxr", f"sxr_{ehf}")
                        if ehf >= 2:
                            Q.wait(E, "s_ow", f"ow_{ehf - 2}")
                        E.tensor_tensor(out=o_stage[ehf % 2][:],
                                        in0=go_stage[ehf % 2][:],
                                        in1=s_sl[ehf % 2][:], op=ALU.mult)
                        Q.inc(E.tensor_tensor(out=o_stage[ehf % 2][:],
                                              in0=o_stage[ehf % 2][:],
                                              in1=x_sl[ehf % 2][:],
                                              op=ALU.add),
                              "c_v", f"oeps_{ehf}")
                        ehf += 1

    def body_tensor(E, Q):
        Q.engine("tensor")
        if POISON:
            Q.wait(E, "s_lbw", "poison_done")
        Q.wait(E, "c_v", "ident_bf")
        Q.wait(E, "s_par", "par_core")
        for q in range(NQ):
            if q >= 1:
                Q.wait(E, "c_v", f"dbc_ev_{q - 1}")
            Q.inc(E.matmul(pb[4][0:128, 0:512], ones1[:],
                           dbcast[0:1, q * 512:(q + 1) * 512], start=True,
                           stop=True), "c_p", f"dbc_mm_{q}")
        Q.wait(E, "c_v", f"dbc_ev_{NQ - 1}")
        for dc in range(DC):
            Q.wait(E, "c_v", f"ctile_used_re_{dc}")
            if dc > 0:
                Q.wait(E, "c_v", f"ctre_ev_{dc - 1}")
            Q.inc(E.transpose(vts[0][0:NH, 0:128], ctile_bf[:], ident_bf[:]),
                  "c_p", f"ctp_re_{dc}")
            Q.wait(E, "c_v", f"ctile_used_im_{dc}")
            if dc > 0:
                Q.wait(E, "c_v", f"ctim_ev_{dc - 1}")
            Q.inc(E.transpose(vts[1][0:NH, 0:128], ctile_bf[:], ident_bf[:]),
                  "c_p", f"ctp_im_{dc}")
        Q.wait(E, "c_v", f"ctre_ev_{DC - 1}")
        Q.wait(E, "c_v", f"ctim_ev_{DC - 1}")
        Q.wait(E, "c_v", "bbar_bf")
        for dc in range(DC):
            if dc > 0:
                Q.wait(E, "c_v", f"btp_ev_{dc - 1}")
            E.transpose(vts[0][0:128, 0:NH], bre_bf[:, dc * 128:(dc + 1) * 128],
                        identNH_bf[:])
            Q.inc(E.transpose(vts[1][0:128, 0:NH],
                              bim_bf[:, dc * 128:(dc + 1) * 128],
                              identNH_bf[:]), "c_p", f"btp_{dc}")
        Q.wait(E, "c_v", f"btp_ev_{DC - 1}")
        p0i = 0
        for lb in range(LB):
            own = lb >= CB
            tb = lb - CB
            for g in range(NGR):
                nd = min(4, DC - g * 4)
                Q.wait(E, "c_a", f"xbfq_{lb}_{g}")
                gd = p0guard(p0i)
                if gd is not None:
                    Q.wait(E, "c_v", gd)
                for j in range(nd):
                    inst = E.transpose(vts[g % 2][:, j * 128:(j + 1) * 128],
                                       xbf_q[g % 2][:, j * 128:(j + 1) * 128],
                                       ident_bf[:])
                    if j == nd - 1:
                        Q.inc(inst, "c_p", f"xnT_{lb}_{g}")
                p0i += 1
            Q.wait(E, "c_v", f"xnT_ev_{lb}_{NGR - 1}")
            if lb >= 1:
                Q.wait(E, "c_v", f"bu_ev_{lb - 1}")
            for dc in range(DC):
                E.matmul(pb[4][0:NH, 0:128], bretT[:, dc * NH:(dc + 1) * NH],
                         xnT_tile[lb % 2][:, dc * 128:(dc + 1) * 128],
                         start=(dc == 0), stop=(dc == DC - 1))
                inst = E.matmul(pb[5][0:NH, 0:128],
                                bimtT[:, dc * NH:(dc + 1) * NH],
                                xnT_tile[lb % 2][:, dc * 128:(dc + 1) * 128],
                                start=(dc == 0), stop=(dc == DC - 1))
                if dc == DC - 1:
                    Q.inc(inst, "c_p", f"bu_mm_{lb}")
            if own:
                for g in range(NGR):
                    nd = min(4, DC - g * 4)
                    Q.wait(E, "c_v", f"qch_{tb}_{g}")
                    gd = p0guard(p0i)
                    if gd is not None:
                        Q.wait(E, "c_v", gd)
                    for j in range(nd):
                        inst = E.transpose(
                            vts[g % 2][:, j * 128:(j + 1) * 128],
                            qneg[:, (g * 4 + j) * 128:(g * 4 + j + 1) * 128],
                            ident_bf[:])
                        if j == nd - 1:
                            Q.inc(inst, "c_p", f"xqT_{tb}_{g}")
                    p0i += 1
        Q.wait(E, "c_v", "xs_bf")
        for tb in range(TB):
            if tb >= 1:
                Q.wait(E, "c_v", f"s_asm_{tb - 1}")
            lhs_re = xs_bf[:, tb * 128:(tb + 1) * 128]
            lhs_im = xs_bf[:, T + tb * 128:T + (tb + 1) * 128]
            for q in range(NQ):
                E.matmul(pb[q % 4][:], lhs_re,
                         ctre_bf[:, q * 512:(q + 1) * 512], start=True,
                         stop=False)
                inst = E.matmul(pb[q % 4][:], lhs_im,
                                ctim_bf[:, q * 512:(q + 1) * 512],
                                start=False, stop=True)
                if q == NQ - 1:
                    Q.inc(inst, "c_p", f"y_mm_{tb}")
        Q.wait(E, "c_v", f"wstacc_{NW_GU - 1}")
        Q.wait(E, "c_v", f"s_asm_{TB - 1}")
        for i, acc in enumerate([wsum_g, wsum_u]):
            inst = E.matmul(pb[5][0:1, i:i + 1], acc[:], ones128[:],
                            start=True, stop=True)
            if i == 1:
                Q.inc(inst, "c_p", "wtot_mm")
        Q.wait(E, "c_v", "wtot_ev")
        Q.inc(E.matmul(pb[4][0:128, 0:2], ones1[:], sc11[0:1, 0:2],
                       start=True, stop=True), "c_p", "wtot_bc")
        for ic in range(ICN):
            Q.wait(E, "c_v", "sw_ready")
            Q.wait(E, "s_lnw", f"lnw_{ic}")
            if ic >= 1:
                Q.wait(E, "c_v", f"lnwq_cp_{ic - 1}")
            Q.inc(E.matmul(pb[4][0:128, 0:512], ones1[:], lnw_sl[ic % 2][:],
                           start=True, stop=True), "c_p", f"lnwq_mm_{ic}")
            for mi in range(2):
                for q in range(QN):
                    Q.wait(E, "c_v", f"tern_gu_{(ic * 2 + mi) * QN + q}")
            Q.wait(E, "c_v", f"cgu_{TB - 1}")
            for tb in range(TB):
                gi = ic * TB + tb
                if gi >= 2:
                    Q.wait(E, "c_a", f"gsilu_{gi - 2}")
                    Q.wait(E, "c_v", f"h_{(gi - 2) // TB}_{(gi - 2) % TB}")
                for dc in range(DC):
                    E.matmul(pb[0 + gi % 2][:],
                             x_intT[:, dc * T + tb * 128: dc * T + (tb + 1) * 128],
                             wg_bf[ic % 2][:, dc * 512:(dc + 1) * 512],
                             start=(dc == 0), stop=(dc == DC - 1))
                for dc in range(DC):
                    inst = E.matmul(pb[2 + gi % 2][:],
                                    x_intT[:, dc * T + tb * 128: dc * T + (tb + 1) * 128],
                                    wu_bf[ic % 2][:, dc * 512:(dc + 1) * 512],
                                    start=(dc == 0), stop=(dc == DC - 1))
                    if dc == DC - 1:
                        Q.inc(inst, "c_p", f"gu_mm_{gi}")
        Q.wait(E, "c_v", f"wstdacc_{NW_D - 1}")
        Q.inc(E.matmul(pb[5][0:1, 2:3], wsum_d[:], ones128[:], start=True,
                       stop=True), "c_p", "wtotd_mm")
        Q.wait(E, "c_v", "wtotd_ev")
        Q.wait(E, "c_v", f"lnwq_ev_{ICN - 1}")
        Q.inc(E.matmul(pb[4][0:128, 0:1], ones1[:], sc11[0:1, 2:3],
                       start=True, stop=True), "c_p", "wtotd_bc")
        for sb in range(NSB):
            nch = TBS * HCH
            for ch in range(nch):
                gi = sb * nch + ch
                Q.wait(E, "c_v", f"hq_done_{gi}")
                if sb > 0 and ch == 0:
                    Q.wait(E, "c_v", f"oeps_{sb * HH * TBS * NHF - 1}")
                for g in range(HGR):
                    if g >= 2:
                        Q.wait(E, "c_v", f"qhT_ev_{gi}_{g - 2}")
                    elif gi > 0:
                        Q.wait(E, "c_v", f"qhT_ev_{gi - 1}_{HGR - 2 + g}")
                    for j in range(4):
                        inst = E.transpose(
                            vts[g % 2][:, j * 128:(j + 1) * 128],
                            qh_tok[gi % 2][:, (g * 4 + j) * 128:(g * 4 + j + 1) * 128],
                            ident_bf[:])
                        if j == 3:
                            Q.inc(inst, "c_p", f"qhT_{gi}_{g}")
            for hh in range(HH):
                Q.wait(E, "c_v", f"qhT_ev_{sb * nch + nch - 1}_{HGR - 1}")
                if sb == 0 and hh == 0:
                    Q.wait(E, "c_a", f"gsilu_{ICN * TB - 1}")
                if sb * HH + hh >= 1:
                    Q.wait(E, "c_v",
                           f"oeps_{(sb * HH + hh) * TBS * NHF - 1}")
                for icI in range(ICI):
                    wi = (sb * HH + hh) * ICI + icI
                    Q.wait(E, "s_wdb", f"wdb_{wi}")
                    last = None
                    for tbp in range(TBS):
                        for half in range(NHF):
                            last = E.matmul(
                                pb[NHF * tbp + half][:],
                                qhT[:, icI * TBS * 128 + tbp * 128: icI * TBS * 128 + (tbp + 1) * 128],
                                wdb[wi % NWDB][:, half * 512:(half + 1) * 512],
                                start=(icI == 0), stop=(icI == ICI - 1))
                    Q.inc(last, "c_p", f"dmm_{wi}")

    bodies = [body_sync, body_gpsimd, body_scalar, body_vector, body_tensor]
    dry = P(tracks, emit=False)
    mock = _MockEng()
    for b in bodies:
        b(mock, dry)
    dry.cp.update({"TOTAL_" + t.name: t.n for t in tracks})
    _check_deadlock(dry)

    for t in tracks:
        t.n = 0
    emitP = P(tracks, emit=True, frozen=dry.cp)
    with contextlib.ExitStack() as es:
        for t in tracks:
            t.sem = es.enter_context(nc.semaphore(t.name))
        with nc.Block() as block:
            block.sync(lambda e: body_sync(e, emitP))
            block.gpsimd(lambda e: body_gpsimd(e, emitP))
            block.scalar(lambda e: body_scalar(e, emitP))
            block.vector(lambda e: body_vector(e, emitP))
            block.tensor(lambda e: body_tensor(e, emitP))

    _PROG_CACHE[key] = nc
    return nc


def make_in_maps(cfg, u, wg, wu, wd, lnw, bre, bim, cre, cim, dv, lunc, lim,
                 logd, n_cores=8):
    D, T, CTX, NH = cfg["D"], cfg["T"], cfg["CTX"], cfg["NH"]
    shared = dict(
        wgT=np.ascontiguousarray(np.asarray(wg, np.float32).T),
        wuT=np.ascontiguousarray(np.asarray(wu, np.float32).T),
        wdT=np.ascontiguousarray(np.asarray(wd, np.float32).T),
        lnw_d=np.ascontiguousarray(np.asarray(lnw, np.float32).reshape(1, -1)),
        bre_d=np.ascontiguousarray(np.asarray(bre, np.float32)),
        bim_d=np.ascontiguousarray(np.asarray(bim, np.float32)),
        cre_d=np.ascontiguousarray(np.asarray(cre, np.float32)),
        cim_d=np.ascontiguousarray(np.asarray(cim, np.float32)),
        dvec_d=np.ascontiguousarray(np.asarray(dv, np.float32).reshape(1, -1)),
        lunc_d=np.ascontiguousarray(
            np.asarray(lunc, np.float32).reshape(-1, 1)),
        lim_d=np.ascontiguousarray(np.asarray(lim, np.float32).reshape(-1, 1)),
        logd_d=np.ascontiguousarray(
            np.asarray(logd, np.float32).reshape(-1, 1)[:NH]),
    )
    u = np.asarray(u, np.float32)
    per_seq = u.shape[1] // T
    in_maps = []
    for c in range(n_cores):
        b, h = c // per_seq, c % per_seq
        m = dict(shared)
        m["u_own"] = np.ascontiguousarray(u[b, h * T:(h + 1) * T, :])
        m["u_ctx"] = (np.zeros((CTX, D), np.float32) if h == 0 else
                      np.ascontiguousarray(u[b, h * T - CTX:h * T, :]))
        in_maps.append(m)
    return in_maps


def kernel(L0_attn_u, L0_attn_Lambda_unconstrained_re, L0_attn_Lambda_im,
           L0_attn_B_re, L0_attn_B_im, L0_attn_C_re, L0_attn_C_im, L0_attn_D,
           L0_attn_log_Delta, L0_ffn_w_gate, L0_ffn_w_up, L0_ffn_w_down,
           L0_ffn_ln_weight):
    cfg = FULL_CFG
    D, T = cfg["D"], cfg["T"]
    u = np.asarray(L0_attn_u, np.float32)
    B, Lseq, _ = u.shape
    nc = build_program(cfg)
    in_maps = make_in_maps(cfg, u, L0_ffn_w_gate, L0_ffn_w_up, L0_ffn_w_down,
                           L0_ffn_ln_weight, L0_attn_B_re, L0_attn_B_im,
                           L0_attn_C_re, L0_attn_C_im, L0_attn_D,
                           L0_attn_Lambda_unconstrained_re, L0_attn_Lambda_im,
                           L0_attn_log_Delta, n_cores=8)
    res = run_bass_kernel_spmd(nc, in_maps, core_ids=list(range(8)))
    out = np.empty((B, Lseq, D), np.float32)
    per_seq = Lseq // T
    for c in range(8):
        b, h = c // per_seq, c % per_seq
        out[b, h * T:(h + 1) * T, :] = res.results[c]["out"]
    return out
